# version 19
# baseline (speedup 1.0000x reference)
import numpy as np
import ml_dtypes
from contextlib import ExitStack

import concourse.bass as bass
import concourse.mybir as mybir
from concourse.bass_utils import run_bass_kernel_spmd

F32 = mybir.dt.float32
BF16 = mybir.dt.bfloat16
I32 = mybir.dt.int32
AF = mybir.ActivationFunctionType
ALU = mybir.AluOpType
AX = mybir.AxisListType

P = 128
T = 4096
D = 1024
NT = T // P
NF = D // P
NH = 16
HD = 64
NPAIR = 8
EPS = 1e-5
BF = ml_dtypes.bfloat16

ENGINES = ("tensor", "scalar", "vector", "gpsimd", "sync")


class Sched:
    def __init__(self, nc, E):
        self.nc = nc
        self.E = E
        self.steps = []
        self.res = {}

    def step(self, engine, fn, reads=(), writes=(), dma=False, tag=""):
        sid = len(self.steps)
        deps = {}
        for r in reads:
            e = self.res.get(r)
            if e and e["w"] is not None:
                deps[e["w"]] = "raw"
        for w in writes:
            e = self.res.get(w)
            if e:
                if e["w"] is not None and deps.get(e["w"]) != "raw":
                    deps[e["w"]] = "waw"
                for rd in e["r"]:
                    deps.setdefault(rd, "war")
        for r in reads:
            self.res.setdefault(r, {"w": None, "r": []})["r"].append(sid)
        for w in writes:
            self.res[w] = {"w": sid, "r": []}
        self.steps.append(dict(id=sid, eng=engine, fn=fn, deps=deps, dma=dma,
                               need_mark=dma, mark=None, tag=tag))
        return sid

    @staticmethod
    def _dep_needs_sync(ds, st, kind):
        if ds["dma"]:
            return True
        if ds["eng"] != st["eng"]:
            return True
        return kind in ("raw", "waw") and st["eng"] != "tensor"

    def emit(self, limit=None):
        import os
        if limit is None:
            limit = int(os.environ.get("K_STEP_LIMIT", "0")) or None
        steps = self.steps if limit is None else self.steps[:limit]
        for st in steps:
            for d, kind in st["deps"].items():
                ds = steps[d]
                if self._dep_needs_sync(ds, st, kind):
                    ds["need_mark"] = True
        NP = 8
        counts = {}
        dmak = {}
        for st in steps:
            if st["dma"]:
                q = st["eng"]
                k = dmak.get(q, 0)
                dmak[q] = k + 1
                key = ("dmap", q, k % NP)
                st["mark"] = (key, 16 * (k // NP + 1))
                st["throttle"] = (key, 16 * (k // NP))
                counts[key] = 16 * (k // NP + 1)
            elif st["need_mark"]:
                k = ("e", st["eng"])
                counts[k] = counts.get(k, 0) + 1
                st["mark"] = (k, counts[k])
        sems = {k: self.E(self.nc.semaphore("s_" + "_".join(str(x) for x in k))) for k in counts}
        self._bsem = {en: self.E(self.nc.semaphore(f"bz_{en}"))
                      for en in ("vector", "scalar", "gpsimd")}
        self._bcnt = {en: 0 for en in self._bsem}
        block = self.E(self.nc.Block())
        for eng_name in ENGINES:
            mine = [st for st in steps if st["eng"] == eng_name]
            if not mine:
                continue

            def body(e, mine=mine, eng_name=eng_name):
                waited = {}
                self._cur_eng = eng_name
                for st in mine:
                    for d in sorted(st["deps"]):
                        kind = st["deps"][d]
                        ds = steps[d]
                        if ds["mark"] is None:
                            continue
                        if not self._dep_needs_sync(ds, st, kind):
                            continue
                        key, v = ds["mark"]
                        if waited.get(key, 0) >= v:
                            continue
                        e.wait_ge(sems[key], v)
                        waited[key] = v
                    if st["dma"]:
                        tk, tv = st["throttle"]
                        if tv > 0 and waited.get(tk, 0) < tv:
                            e.wait_ge(sems[tk], tv)
                            waited[tk] = tv
                    instr = st["fn"](e)
                    if st["mark"] is not None:
                        key = st["mark"][0]
                        instr.then_inc(sems[key], 16 if st["dma"] else 1)

            getattr(block, eng_name)(body)

    def bz(self, e, instr):
        en = self._cur_eng
        self._bcnt[en] += 1
        instr.then_inc(self._bsem[en], 1)
        e.wait_ge(self._bsem[en], self._bcnt[en])
        return instr

    def dma(self, q, out, in_, reads=(), writes=(), tag="", **kw):
        extra = []
        dd = getattr(self, "dram_deps", None)
        if dd and getattr(in_, "tensor", None) is not None:
            nm = getattr(in_.tensor, "name", "")
            if nm in dd:
                extra = dd[nm]
        return self.step(q, lambda e: e.dma_start(out=out, in_=in_, **kw),
                         reads=list(reads) + extra, writes=writes, dma=True, tag=tag)


def host_constants():
    c = {}
    c["CIDENT"] = np.eye(P, dtype=np.float32)
    c["CONESROW"] = np.ones((1, P), dtype=np.float32)
    c["CIDENTB"] = np.eye(P, dtype=np.float32).astype(BF)
    d = HD
    freqs = (1.0 / (10000.0 ** (np.arange(0, d, 2, dtype=np.float32) / float(d)))).astype(np.float32)
    ang = np.arange(T, dtype=np.float32)[:, None] * freqs[None, :]
    C1 = np.cos(ang).astype(np.float32).T[np.repeat(np.arange(32), 2)]
    S1 = np.sin(ang).astype(np.float32).T[np.repeat(np.arange(32), 2)]
    c["CROPE_C"] = np.ascontiguousarray(np.vstack([C1, C1])).astype(BF)
    c["CROPE_S"] = np.ascontiguousarray(np.vstack([S1, S1])).astype(BF)
    perm = np.zeros((P, P), dtype=np.float32)
    for blk in range(2):
        o = blk * 64
        for i in range(32):
            perm[o + 2 * i + 1, o + 2 * i] = -1.0
            perm[o + 2 * i, o + 2 * i + 1] = 1.0
    c["CROPE_PERM"] = perm.astype(BF)
    return c


def mk_psum_pool(nc, E):
    return {"SPa": E(nc.psum_tensor("ps_spa", [P, 1024], F32)),
            "SPb": E(nc.psum_tensor("ps_spb", [P, 1024], F32)),
            "M0": E(nc.psum_tensor("ps_m0", [P, 512], F32)),
            "M1": E(nc.psum_tensor("ps_m1", [P, 512], F32)),
            "M2": E(nc.psum_tensor("ps_m2", [P, 512], F32)),
            "M3": E(nc.psum_tensor("ps_m3", [P, 512], F32))}


def emit_ln(S, nc, E, pre, x_tiles, x_keys, n_tiles, x_out, xbf_out, out_keys,
            sq_ps, sq_key, st):
    nt = n_tiles
    ki = f"{pre}_lnstat"
    for i in range(n_tiles):
        xt, xk = x_tiles[i], x_keys[i]
        S.step("vector", lambda e, xt=xt, st=st, i=i: e.reduce_sum(
            st[:, i:i + 1], xt[:], axis=AX.X),
            reads=[xk], writes=[(ki, "s1", i)], tag=f"{pre}_ln_s1_{i}")
        S.step("scalar", lambda e, xt=xt, st=st, i=i: e.activation(
            sq_ps[:, 0:512], xt[:, 0:512], AF.Square, accum_out=st[:, 8 + i:9 + i]),
            reads=[xk], writes=[sq_key, (ki, "s2a", i)], tag=f"{pre}_ln_sqa_{i}")
        S.step("scalar", lambda e, xt=xt, st=st, i=i: e.activation(
            sq_ps[:, 0:512], xt[:, 512:1024], AF.Square, accum_out=st[:, 16 + i:17 + i]),
            reads=[xk], writes=[sq_key, (ki, "s2b", i)], tag=f"{pre}_ln_sqb_{i}")

    def var_chain(e, st=st, nt=nt):
        S.bz(e, e.tensor_scalar(st[:, 24:24 + nt], st[:, 0:nt], float(1.0 / D), None, ALU.mult))
        S.bz(e, e.tensor_tensor(st[:, 32:32 + nt], st[:, 8:8 + nt], st[:, 16:16 + nt], ALU.add))
        S.bz(e, e.tensor_scalar(st[:, 32:32 + nt], st[:, 32:32 + nt], float(1.0 / D), None, ALU.mult))
        S.bz(e, e.tensor_tensor(st[:, 40:40 + nt], st[:, 24:24 + nt], st[:, 24:24 + nt], ALU.mult))
        S.bz(e, e.tensor_tensor(st[:, 32:32 + nt], st[:, 32:32 + nt], st[:, 40:40 + nt], ALU.subtract))
        return e.tensor_scalar(st[:, 32:32 + nt], st[:, 32:32 + nt], float(EPS), None, ALU.add)

    allsums = [(ki, t, i) for t in ("s1", "s2a", "s2b") for i in range(nt)]
    S.step("vector", var_chain, reads=allsums, writes=[(ki, "var")],
           tag=f"{pre}_ln_var")

    def rstd_chain(e, st=st, nt=nt):
        S.bz(e, e.activation(st[:, 40:40 + nt], st[:, 32:32 + nt], AF.Ln))
        return e.activation(st[:, 40:40 + nt], st[:, 40:40 + nt], AF.Exp, scale=float(-0.5))

    S.step("scalar", rstd_chain, reads=[(ki, "var")], writes=[(ki, "rstd")],
           tag=f"{pre}_ln_rstd")
    for i in range(n_tiles):
        xt, xk = x_tiles[i], x_keys[i]
        S.step("vector", lambda e, xt=xt, st=st, i=i: e.tensor_scalar(
            xt[:], xt[:], st[:, 24 + i:25 + i], st[:, 40 + i:41 + i],
            ALU.subtract, ALU.mult),
            reads=[(ki, "rstd"), (ki, "var"), xk], writes=[xk],
            tag=f"{pre}_ln_norm_{i}")
        S.dma("gpsimd", x_out[i * P:(i + 1) * P, :], xt[:],
              reads=[xk], writes=[out_keys[i][0]], tag=f"{pre}_st_x_{i}")
        S.dma("gpsimd", xbf_out[i * P:(i + 1) * P, :], xt[:],
              reads=[xk], writes=[out_keys[i][1]], tag=f"{pre}_st_xbf_{i}")


def emit_l1ffn_quant(S, nc, E, aps, ps, KQ_d, pre="gq"):
    ksb = E(nc.sbuf_tensor(f"{pre}_ksb", [P, 8 * D], F32))
    kq = E(nc.sbuf_tensor(f"{pre}_kq", [P, 8 * D], BF16))
    qtmp = E(nc.sbuf_tensor(f"{pre}_qtmp", [P, D], F32))
    qtmpi = E(nc.sbuf_tensor(f"{pre}_qtmpi", [P, D], I32))
    stat = E(nc.sbuf_tensor(f"{pre}_stat", [P, 16], F32))
    onesr = E(nc.sbuf_tensor(f"{pre}_ones", [1, P], F32))
    ident = E(nc.sbuf_tensor(f"{pre}_ident", [P, P], F32))
    tp_ps = ps["M1"]
    K_ksb, K_stat, K_kq, K_tmp = f"{pre}ksb", f"{pre}stat", f"{pre}kq", f"{pre}tmp"
    S.dma("sync", ksb[:].rearrange("p (c e) -> p c e", e=D),
          aps["KER"].rearrange("(c p) e -> p c e", p=P), writes=[K_ksb],
          tag=f"{pre}_ldker")
    S.dma("sync", onesr[:], aps["CONESROW"], writes=[f"{pre}ones"])
    S.dma("sync", ident[:], aps["CIDENT"], writes=[f"{pre}ident"])

    def qmax(e):
        for c in range(NF):
            ins = e.reduce_max(stat[:, c:c + 1], ksb[:, c * D:(c + 1) * D], axis=AX.X,
                               apply_absolute_value=True)
            if c == NF - 1:
                S.bz(e, ins)
        return e.reduce_max(stat[:, 8:9], stat[:, 0:8], axis=AX.X)

    S.step("vector", qmax, reads=[K_ksb], writes=[K_stat], tag=f"{pre}_qmax")
    S.step("tensor", lambda e: e.transpose(tp_ps[0:1, 0:P], stat[:, 8:9], ident[:]),
           reads=[K_stat, f"{pre}ident"], writes=["PS_M1"], tag=f"{pre}_qmax_tp")

    def qscale(e):
        S.bz(e, e.tensor_copy(qtmp[0:1, 0:P], tp_ps[0:1, 0:P]))
        S.bz(e, e.reduce_max(stat[0:1, 9:10], qtmp[0:1, 0:P], axis=AX.X))
        e.tensor_scalar(stat[0:1, 10:11], stat[0:1, 9:10], float(1.0 / 7.0), None, ALU.mult)
        S.bz(e, e.reciprocal(stat[0:1, 11:12], stat[0:1, 9:10]))
        return e.tensor_scalar(stat[0:1, 11:12], stat[0:1, 11:12], float(7.0), None, ALU.mult)

    S.step("vector", qscale, reads=["PS_M1"], writes=[K_stat, K_tmp], tag=f"{pre}_qscale")
    S.step("tensor", lambda e: e.matmul(tp_ps[:, 0:2], onesr[:], stat[0:1, 10:12],
                                        start=True, stop=True),
           reads=[K_stat, f"{pre}ones"], writes=["PS_M1"], tag=f"{pre}_qsc_bc")
    S.step("vector", lambda e: e.tensor_copy(stat[:, 12:14], tp_ps[:, 0:2]),
           reads=["PS_M1"], writes=[K_stat], tag=f"{pre}_qsc_ev")
    outk = []
    for c in range(NF):
        def quant(e, c=c):
            sl = ksb[:, c * D:(c + 1) * D]
            S.bz(e, e.tensor_scalar(qtmp[:], sl, stat[:, 13:14], None, ALU.mult))
            S.bz(e, e.tensor_copy(qtmpi[:], qtmp[:]))
            S.bz(e, e.tensor_copy(qtmp[:], qtmpi[:]))
            S.bz(e, e.tensor_scalar(qtmp[:], qtmp[:], float(-8.0), float(7.0),
                                    ALU.max, ALU.min))
            return e.tensor_scalar(kq[:, c * D:(c + 1) * D], qtmp[:], stat[:, 12:13],
                                   None, ALU.mult)

        S.step("vector", quant, reads=[K_ksb, K_stat], writes=[(K_kq, c), K_tmp],
               tag=f"{pre}_quant_{c}")
        k = (f"{pre}kqd", c)
        outk.append(k)
        S.dma("gpsimd", KQ_d[:, c * D:(c + 1) * D], kq[:, c * D:(c + 1) * D],
              reads=[(K_kq, c)], writes=[k], tag=f"{pre}_st_{c}")
    return outk


def build_l1ffn(S, nc, E, aps, ps, pre="l1f", dbg=None):
    SPAN = 1024
    NSP = T // SPAN
    TPS = SPAN // P

    ksb = E(nc.sbuf_tensor(f"{pre}_ksb", [P, 8 * D], F32))
    kq = E(nc.sbuf_tensor(f"{pre}_kq", [P, 8 * D], BF16))
    qtmp = E(nc.sbuf_tensor(f"{pre}_qtmp", [P, D], F32))
    qtmpi = E(nc.sbuf_tensor(f"{pre}_qtmpi", [P, D], I32))
    stat = E(nc.sbuf_tensor(f"{pre}_stat", [P, 16], F32))
    onesr = E(nc.sbuf_tensor(f"{pre}_ones", [1, P], F32))
    brow = E(nc.sbuf_tensor(f"{pre}_brow", [1, D], F32))
    bias_bc = E(nc.sbuf_tensor(f"{pre}_biasbc", [P, D], F32))
    abf = E(nc.sbuf_tensor(f"{pre}_abf", [P, NF * 16], BF16))
    bbf = E(nc.sbuf_tensor(f"{pre}_bbf", [16, D], BF16))
    xsp = [E(nc.sbuf_tensor(f"{pre}_xsp{i}", [P, D], F32)) for i in range(TPS)]
    xTs = [E(nc.sbuf_tensor(f"{pre}_xT{c}", [P, SPAN], BF16)) for c in range(NF)]
    t1bf = E(nc.sbuf_tensor(f"{pre}_t1", [16, SPAN], BF16))
    mm_ps = [ps["SPa"], ps["SPb"]]
    t1_ps = ps["M0"]
    tp_ps = ps["M1"]
    sq_ps = ps["M2"]
    ident = E(nc.sbuf_tensor(f"{pre}_ident", [P, P], F32))

    K_ksb, K_kq, K_stat, K_tp = f"{pre}ksb", f"{pre}kq", f"{pre}stat", "PS_M1"
    K_tmp = f"{pre}tmp"

    S.dma("sync", brow[:], aps["BIAS"][None, :], writes=[f"{pre}brow"])
    S.dma("sync", onesr[:], aps["CONESROW"], writes=[f"{pre}ones"])
    S.dma("sync", ident[:], aps["CIDENT"], writes=[f"{pre}ident"])
    S.dma("gpsimd", abf[:].rearrange("p (c r) -> p c r", r=16),
          aps["Aw"].rearrange("(c p) r -> p c r", p=P),
          writes=[f"{pre}abf"])
    S.dma("gpsimd", bbf[:], aps["Bw"], writes=[f"{pre}bbf"])
    if aps.get("KQ_PREBUILT"):
        KQ_d, kqdk = aps["KQ_PREBUILT"]
        S.dma("sync", kq[:], KQ_d, reads=kqdk, writes=[(K_kq, c) for c in range(NF)],
              tag="ld_kq")
    else:
        S.dma("sync", ksb[:].rearrange("p (c e) -> p c e", e=D),
              aps["KER"].rearrange("(c p) e -> p c e", p=P),
              writes=[K_ksb], tag="ld_ker")

        def qmax(e):
            for c in range(NF):
                ins = e.reduce_max(stat[:, c:c + 1], ksb[:, c * D:(c + 1) * D], axis=AX.X,
                                   apply_absolute_value=True)
                if c == NF - 1:
                    S.bz(e, ins)
            return e.reduce_max(stat[:, 8:9], stat[:, 0:8], axis=AX.X)

        S.step("vector", qmax, reads=[K_ksb], writes=[K_stat], tag="qmax")
        S.step("tensor", lambda e: e.transpose(tp_ps[0:1, 0:P], stat[:, 8:9], ident[:]),
               reads=[K_stat, f"{pre}ident"], writes=[K_tp], tag="qmax_tp")

        def qscale(e):
            S.bz(e, e.tensor_copy(qtmp[0:1, 0:P], tp_ps[0:1, 0:P]))
            S.bz(e, e.reduce_max(stat[0:1, 9:10], qtmp[0:1, 0:P], axis=AX.X))
            e.tensor_scalar(stat[0:1, 10:11], stat[0:1, 9:10], float(1.0 / 7.0), None, ALU.mult)
            S.bz(e, e.reciprocal(stat[0:1, 11:12], stat[0:1, 9:10]))
            return e.tensor_scalar(stat[0:1, 11:12], stat[0:1, 11:12], float(7.0), None, ALU.mult)

        S.step("vector", qscale, reads=[K_tp], writes=[K_stat, K_tmp], tag="qscale")
        S.step("tensor", lambda e: e.matmul(tp_ps[:, 0:2], onesr[:], stat[0:1, 10:12],
                                            start=True, stop=True),
               reads=[K_stat, f"{pre}ones"], writes=[K_tp], tag="qscale_bc")
        S.step("vector", lambda e: e.tensor_copy(stat[:, 12:14], tp_ps[:, 0:2]),
               reads=[K_tp], writes=[K_stat], tag="qscale_ev")

        for c in range(NF):
            def quant(e, c=c):
                sl = ksb[:, c * D:(c + 1) * D]
                S.bz(e, e.tensor_scalar(qtmp[:], sl, stat[:, 13:14], None, ALU.mult))
                S.bz(e, e.tensor_copy(qtmpi[:], qtmp[:]))
                S.bz(e, e.tensor_copy(qtmp[:], qtmpi[:]))
                S.bz(e, e.tensor_scalar(qtmp[:], qtmp[:], float(-8.0), float(7.0),
                                        ALU.max, ALU.min))
                return e.tensor_scalar(kq[:, c * D:(c + 1) * D], qtmp[:], stat[:, 12:13],
                                       None, ALU.mult)

            S.step("vector", quant, reads=[K_ksb, K_stat], writes=[(K_kq, c), K_tmp],
                   tag=f"quant_{c}")

        if dbg is not None:
            S.dma("gpsimd", dbg["KQ"], kq[:], reads=[(K_kq, c) for c in range(NF)],
                  writes=["dbg1"], tag="dbg_kq")
            S.dma("gpsimd", dbg["STAT"], stat[:], reads=[K_stat], writes=["dbg2"], tag="dbg_stat")

    S.step("tensor", lambda e: e.matmul(tp_ps[:, 0:P], onesr[:], brow[0:1, 0:P],
                                        start=True, stop=True),
           reads=[f"{pre}brow", f"{pre}ones"], writes=[K_tp], tag="bb0")
    S.step("vector", lambda e: e.tensor_copy(bias_bc[:, 0:P], tp_ps[:, 0:P]),
           reads=[K_tp], writes=[(f"{pre}bbc", 0)], tag="bb0e")
    for part in range(1, NF):
        S.step("tensor", lambda e, part=part: e.matmul(
            tp_ps[:, 0:P], onesr[:], brow[0:1, part * P:(part + 1) * P],
            start=True, stop=True), reads=[f"{pre}brow"], writes=[K_tp],
            tag=f"bb{part}")
        S.step("vector", lambda e, part=part: e.tensor_copy(
            bias_bc[:, part * P:(part + 1) * P], tp_ps[:, 0:P]),
            reads=[K_tp], writes=[(f"{pre}bbc", part)], tag=f"bb{part}e")
    K_bbc = [(f"{pre}bbc", i) for i in range(NF)]

    all_out = []
    lnstat = [E(nc.sbuf_tensor(f"{pre}_lnstat{b}", [P, 48], F32)) for b in range(2)]
    for sp in range(NSP):
        r0 = sp * SPAN
        xk = [(f"{pre}xsp", i) for i in range(TPS)]
        xtk = [(f"{pre}xT", c) for c in range(NF)]
        for i in range(TPS):
            S.dma("sync", xsp[i][:], aps["X"][r0 + i * P: r0 + (i + 1) * P, :],
                  writes=[xk[i]], reads=[], tag=f"ldx_{sp}_{i}")
        for c in range(NF):
            S.dma("sync", xTs[c][:], aps["XBF"][r0:r0 + SPAN, c * P:(c + 1) * P],
                  writes=[xtk[c]], tag=f"ldxT_{sp}_{c}", transpose=True)
        for i in range(TPS):
            S.step("vector", lambda e, i=i: e.tensor_tensor(xsp[i][:], xsp[i][:],
                                                            bias_bc[:], ALU.add),
                   reads=[xk[i]] + K_bbc, writes=[xk[i]], tag=f"xb_{sp}_{i}")
        if dbg is not None and sp == 0:
            S.dma("gpsimd", dbg["XSP0"], xsp[0][:], reads=[xk[0]], writes=["dbg3"], tag="dbg_xsp")
            S.dma("gpsimd", dbg["XT0"], xTs[0][:], reads=[xtk[0]], writes=["dbg4"], tag="dbg_xt")
            S.dma("gpsimd", dbg["BBC"], bias_bc[:], reads=K_bbc, writes=["dbg5"], tag="dbg_bbc")
        for g in range(SPAN // 512):
            def lora1(e, g=g):
                last = None
                for c in range(NF):
                    last = e.matmul(t1_ps[0:16, :], abf[:, c * 16:(c + 1) * 16],
                                    xTs[c][:, g * 512:(g + 1) * 512],
                                    start=(c == 0), stop=(c == NF - 1))
                return last

            kt1ps = "PS_M0"
            S.step("tensor", lora1, reads=xtk + [f"{pre}abf"], writes=[kt1ps],
                   tag=f"lora1_{sp}_{g}")
            S.step("vector", lambda e, g=g: e.tensor_copy(
                t1bf[:, g * 512:(g + 1) * 512], t1_ps[0:16, :]),
                reads=[kt1ps], writes=[(f"{pre}t1bf", g)], tag=f"lora1e_{sp}_{g}")
        kt1 = [(f"{pre}t1bf", g) for g in range(SPAN // 512)]
        for i in range(TPS):
            ps = mm_ps[i % 2]
            kps_l = [("PS_SPa", 0), ("PS_SPa", 1)] if i % 2 == 0 else [("PS_SPb", 0), ("PS_SPb", 1)]

            def main_mm(e, i=i, ps=ps):
                last = None
                for eh in range(2):
                    for c in range(NF):
                        e.matmul(ps[:, eh * 512:eh * 512 + 512],
                                 xTs[c][:, i * P:(i + 1) * P],
                                 kq[:, c * D + eh * 512: c * D + eh * 512 + 512],
                                 start=(c == 0), stop=False)
                    last = e.matmul(ps[:, eh * 512:eh * 512 + 512],
                                    t1bf[:, i * P:(i + 1) * P],
                                    bbf[:, eh * 512:eh * 512 + 512],
                                    start=False, stop=True)
                return last

            S.step("tensor", main_mm,
                   reads=xtk + kt1 + [(K_kq, c) for c in range(NF)] + [f"{pre}bbf"],
                   writes=kps_l, tag=f"mm_{sp}_{i}")
            S.step("vector", lambda e, i=i, ps=ps: e.tensor_tensor(
                xsp[i][:], xsp[i][:], ps[:], ALU.add),
                reads=kps_l + [xk[i]], writes=[xk[i]], tag=f"acc_{sp}_{i}")
        if dbg is not None and sp == 0:
            S.dma("gpsimd", dbg["T1"], t1bf[:], reads=kt1, writes=["dbg6"], tag="dbg_t1")
            S.dma("gpsimd", dbg["PRELN"], xsp[0][:], reads=[xk[0]], writes=["dbg7"], tag="dbg_preln")
        okeys = [((f"{pre}xo", sp, i), (f"{pre}xbo", sp, i)) for i in range(TPS)]
        all_out += [k for pair in okeys for k in pair]
        emit_ln(S, nc, E, f"{pre}{sp % 2}", xsp, xk, TPS,
                aps["X_OUT"][r0:r0 + SPAN, :], aps["XBF_OUT"][r0:r0 + SPAN, :],
                okeys, sq_ps, "PS_M2", lnstat[sp % 2])

    return all_out


def _build_block_program(block_fn, input_specs, extra_consts=True):
    nc = bass.Bass("TRN2", target_bir_lowering=False)
    stack = ExitStack()
    aps = {}
    for name, shape, dt in input_specs:
        aps[name] = nc.dram_tensor(name, list(shape), dt, kind="ExternalInput").ap()
    aps["X_OUT"] = nc.dram_tensor("X_OUT", [T, D], F32, kind="ExternalOutput").ap()
    aps["XBF_OUT"] = nc.dram_tensor("XBF_OUT", [T, D], BF16, kind="ExternalOutput").ap()
    with stack:
        E = stack.enter_context
        S = Sched(nc, E)
        outk = block_fn(S, nc, E, aps)
        S.step("sync", lambda e: e.nop(), reads=list(outk), tag="fence")
        S.emit()
    return nc


L1FFN_SPECS = [("X", (T, D), F32), ("XBF", (T, D), BF16),
               ("KER", (D, D), F32), ("BIAS", (D,), F32),
               ("Aw", (D, 16), F32), ("Bw", (16, D), F32),
               ("CONESROW", (1, P), F32), ("CIDENT", (P, P), F32)]


def build_l1ffn_program(debug=False):
    nc = bass.Bass("TRN2", target_bir_lowering=False)
    stack = ExitStack()
    aps = {}
    for name, shape, dt in L1FFN_SPECS:
        aps[name] = nc.dram_tensor(name, list(shape), dt, kind="ExternalInput").ap()
    aps["X_OUT"] = nc.dram_tensor("X_OUT", [T, D], F32, kind="ExternalOutput").ap()
    aps["XBF_OUT"] = nc.dram_tensor("XBF_OUT", [T, D], BF16, kind="ExternalOutput").ap()
    dbg = None
    if debug:
        dbg = {"KQ": nc.dram_tensor("KQ", [P, 8 * D], BF16, kind="ExternalOutput").ap(),
               "STAT": nc.dram_tensor("STAT", [P, 16], F32, kind="ExternalOutput").ap(),
               "XSP0": nc.dram_tensor("XSP0", [P, D], F32, kind="ExternalOutput").ap(),
               "XT0": nc.dram_tensor("XT0", [P, 1024], BF16, kind="ExternalOutput").ap(),
               "BBC": nc.dram_tensor("BBC", [P, D], F32, kind="ExternalOutput").ap(),
               "T1": nc.dram_tensor("T1", [16, 1024], BF16, kind="ExternalOutput").ap(),
               "PRELN": nc.dram_tensor("PRELN", [P, D], F32, kind="ExternalOutput").ap()}
    with stack:
        E = stack.enter_context
        S = Sched(nc, E)
        ps = mk_psum_pool(nc, E)
        outk = build_l1ffn(S, nc, E, aps, ps, dbg=dbg)
        S.step("sync", lambda e: e.nop(), reads=list(outk), tag="fence")
        S.emit()
    return nc




def build_l0attn(S, nc, E, aps, ps, pre="l0a", dbg=None):
    NQG = 8
    QG = 512

    wo = E(nc.sbuf_tensor(f"{pre}_wo", [P, NF * D], BF16))
    borow = E(nc.sbuf_tensor(f"{pre}_bo", [1, D], F32))
    bo_bc = E(nc.sbuf_tensor(f"{pre}_bobc", [P, D], F32))
    onesr = E(nc.sbuf_tensor(f"{pre}_ones", [1, P], F32))
    ident = E(nc.sbuf_tensor(f"{pre}_ident", [P, P], F32))
    qt_sb = [E(nc.sbuf_tensor(f"{pre}_qt{i}", [P, T], BF16)) for i in range(2)]
    kt_sb = [E(nc.sbuf_tensor(f"{pre}_kt{i}", [P, T], BF16)) for i in range(2)]
    v_sb = [E(nc.sbuf_tensor(f"{pre}_v{i}", [P, NT * 160], BF16)) for i in range(2)]
    pt = [E(nc.sbuf_tensor(f"{pre}_pt{i}", [P, 1024], BF16)) for i in range(2)]
    rq = E(nc.sbuf_tensor(f"{pre}_rq", [P, 4], F32))
    ocst = E(nc.sbuf_tensor(f"{pre}_ocst", [P, 256], BF16))
    xsp = [[E(nc.sbuf_tensor(f"{pre}_xsp{b}_{i}", [P, D], F32)) for i in range(4)]
           for b in range(2)]
    otf = [[E(nc.sbuf_tensor(f"{pre}_otf{b}_{c}", [P, QG], BF16)) for c in range(NF)]
           for b in range(2)]
    lnstat = [E(nc.sbuf_tensor(f"{pre}_lnstat{b}", [P, 48], F32)) for b in range(2)]
    inner = ExitStack()
    EB = inner.enter_context
    wq = EB(nc.sbuf_tensor(f"{pre}_wq", [P, NF * D], BF16))
    wk = EB(nc.sbuf_tensor(f"{pre}_wk", [P, NF * D], BF16))
    wv = EB(nc.sbuf_tensor(f"{pre}_wv", [P, NF * D], BF16))
    xTs2 = [[EB(nc.sbuf_tensor(f"{pre}_xT{b}_{c}", [P, QG], BF16)) for c in range(NF)]
            for b in range(2)]
    qt_st = [EB(nc.sbuf_tensor(f"{pre}_qtst{i}", [P, QG], BF16)) for i in range(2)]
    kt_st = [EB(nc.sbuf_tensor(f"{pre}_ktst{i}", [P, QG], BF16)) for i in range(2)]
    v_st = [EB(nc.sbuf_tensor(f"{pre}_vst{i}", [P, P], BF16)) for i in range(2)]

    SPa, SPb = ps["SPa"], ps["SPb"]
    M = [ps[f"M{i}"] for i in range(4)]

    QT_d = nc.dram_tensor(f"{pre}_qtd", [NPAIR * P, T], BF16, kind="Internal").ap()
    KT_d = nc.dram_tensor(f"{pre}_ktd", [NPAIR * P, T], BF16, kind="Internal").ap()
    V_d = nc.dram_tensor(f"{pre}_vd", [T, D], BF16, kind="Internal").ap()
    OC_d = nc.dram_tensor(f"{pre}_ocd", [T, D], BF16, kind="Internal").ap()

    for wbuf, nm in ((wq, "WQ"), (wk, "WK"), (wv, "WV"), (wo, "WO")):
        S.dma("gpsimd", wbuf[:].rearrange("p (c e) -> p c e", e=D),
              aps[nm].rearrange("(c p) e -> p c e", p=P), writes=[f"{pre}{nm}"],
              tag=f"ld{nm}")
    S.dma("sync", borow[:], aps["BO"][None, :], writes=[f"{pre}bo"])
    S.dma("sync", onesr[:], aps["CONESROW"], writes=[f"{pre}ones"])
    S.dma("sync", ident[:], aps["CIDENT"], writes=[f"{pre}ident"])

    for eh in range(2):
        S.step("tensor", lambda e, eh=eh: e.matmul(
            SPa[:, eh * 512:eh * 512 + 512], onesr[:], borow[0:1, eh * 512:eh * 512 + 512],
            start=True, stop=True), reads=[f"{pre}bo", f"{pre}ones"],
            writes=[("PS_SPa", eh)], tag=f"bobc{eh}")
        S.step("vector", lambda e, eh=eh: e.tensor_copy(
            bo_bc[:, eh * 512:eh * 512 + 512], SPa[:, eh * 512:eh * 512 + 512]),
            reads=[("PS_SPa", eh)], writes=[(f"{pre}bobc", eh)], tag=f"bobce{eh}")
    K_bobc = [(f"{pre}bobc", i) for i in range(2)]

    for qg in range(NQG):
        r0 = qg * QG
        xTs = xTs2[qg % 2]
        xtk = [(f"{pre}xT", qg % 2, c) for c in range(NF)]
        for c in range(NF):
            S.dma("sync", xTs[c][:], aps["XBF"][r0:r0 + QG, c * P:(c + 1) * P],
                  writes=[xtk[c]], tag=f"B_xT_{qg}_{c}", transpose=True)
        for p in range(NPAIR):
            for (wbuf, wkey, stg, stgk, dram) in (
                    (wq, f"{pre}WQ", qt_st[p % 2], (f"{pre}qtst", p % 2), QT_d),
                    (wk, f"{pre}WK", kt_st[p % 2], (f"{pre}ktst", p % 2), KT_d)):
                base = SPa if dram is QT_d else SPb
                bk = "PS_SPa" if dram is QT_d else "PS_SPb"
                mm_ps = base[:, (p % 2) * 512:(p % 2) * 512 + 512]
                mk = (bk, p % 2)

                def projmm(e, wbuf=wbuf, p=p, mm_ps=mm_ps, xTs=xTs):
                    last = None
                    for c in range(NF):
                        last = e.matmul(mm_ps, wbuf[:, c * D + p * P: c * D + (p + 1) * P],
                                        xTs[c][:], start=(c == 0), stop=(c == NF - 1))
                    return last

                S.step("tensor", projmm, reads=xtk + [wkey], writes=[mk],
                       tag=f"B_mm_{qg}_{p}")
                S.step("vector", lambda e, stg=stg, mm_ps=mm_ps: e.tensor_copy(
                    stg[:], mm_ps), reads=[mk], writes=[stgk],
                    tag=f"B_ev_{qg}_{p}")
                S.dma("gpsimd", dram[p * P:(p + 1) * P, r0:r0 + QG], stg[:],
                      reads=[stgk], writes=[(f"{pre}qtd" if dram is QT_d else f"{pre}ktd", p, qg)],
                      tag=f"B_st_{qg}_{p}")
            for tci in range(QG // P):
                tc = qg * (QG // P) + tci
                vp = M[2 + tci % 2]
                vk = "PS_M2" if tci % 2 == 0 else "PS_M3"

                def vmm(e, p=p, tci=tci, vp=vp, xTs=xTs):
                    last = None
                    for c in range(NF):
                        last = e.matmul(vp[:, 0:P], xTs[c][:, tci * P:(tci + 1) * P],
                                        wv[:, c * D + p * P: c * D + (p + 1) * P],
                                        start=(c == 0), stop=(c == NF - 1))
                    return last

                S.step("tensor", vmm, reads=xtk + [f"{pre}WV"], writes=[vk],
                       tag=f"B_vmm_{qg}_{p}_{tci}")
                stg = v_st[tci % 2]
                stgk = (f"{pre}vst", tci % 2)
                S.step("vector", lambda e, stg=stg, vp=vp: e.tensor_copy(
                    stg[:], vp[:, 0:P]), reads=[vk], writes=[stgk],
                    tag=f"B_vev_{qg}_{p}_{tci}")
                S.dma("gpsimd", V_d[tc * P:(tc + 1) * P, p * P:(p + 1) * P], stg[:],
                      reads=[stgk], writes=[(f"{pre}vd", p, tc)],
                      tag=f"B_vst_{qg}_{p}_{tci}")

    inner.close()
    for i in range(2):
        S.step("vector", lambda e, i=i: e.memset(
            v_sb[i][:].rearrange("p (c g) -> p c g", g=80)[:, :, 64:65], 1.0),
            writes=[(f"{pre}vones", i)], tag=f"C_vones_{i}")

    for p in range(NPAIR):
        qb, kb, vb = qt_sb[p % 2], kt_sb[p % 2], v_sb[p % 2]
        qbk, kbk, vbk = (f"{pre}qt", p % 2), (f"{pre}kt", p % 2), (f"{pre}v", p % 2)
        S.dma("sync", qb[:], QT_d[p * P:(p + 1) * P, :],
              reads=[(f"{pre}qtd", p, g) for g in range(NQG)], writes=[qbk],
              tag=f"C_ldq_{p}")
        S.dma("sync", kb[:], KT_d[p * P:(p + 1) * P, :],
              reads=[(f"{pre}ktd", p, g) for g in range(NQG)], writes=[kbk],
              tag=f"C_ldk_{p}")
        vreads = [(f"{pre}vd", p, tc) for tc in range(NT)] + [(f"{pre}vones", p % 2)]
        for hl in range(2):
            S.dma("sync", vb[:].rearrange("p (c g) -> p c g", g=80)[:, hl::2, 0:64],
                  V_d[:, p * P + hl * 64: p * P + hl * 64 + 64]
                  .rearrange("(c p) v -> p c v", p=P),
                  reads=vreads, writes=[(vbk, hl)], tag=f"C_ldv_{p}_{hl}")
        for qg in range(NQG):
            q0 = qg * QG
            OH = [M[0 + 2 * (qg % 2)], M[1 + 2 * (qg % 2)]]
            OHK = ["PS_M%d" % (0 + 2 * (qg % 2)), "PS_M%d" % (1 + 2 * (qg % 2))]

            def mk_pv(j):
                ptb = pt[j % 2]
                ptk = (f"{pre}pt", j % 2)

                def pv(e, j=j, ptb=ptb, vb=vb, OH=OH):
                    last = None
                    for hl in range(2):
                        for c in range(4):
                            last = e.matmul(
                                OH[hl][:, c * 65:(c + 1) * 65],
                                ptb[:, hl * 512 + c * P: hl * 512 + (c + 1) * P],
                                vb[:, j * 160 + hl * 80: j * 160 + hl * 80 + 65],
                                start=(j == 0 and c == 0),
                                stop=(j == NT - 1 and c == 3))
                    return last

                S.step("tensor", pv, reads=[ptk, (vbk, 0), (vbk, 1)],
                       writes=OHK, tag=f"C_pv_{p}_{qg}_{j}")

            for j in range(NT):
                SP = SPa if j % 2 == 0 else SPb
                bk = "PS_SPa" if j % 2 == 0 else "PS_SPb"
                spk = [(bk, 0), (bk, 1)]
                ptb = pt[j % 2]
                ptk = (f"{pre}pt", j % 2)

                def smm(e, SP=SP, j=j, qb=qb, kb=kb, q0=q0):
                    e.matmul(SP[:, 0:512], kb[0:64, j * P:(j + 1) * P],
                             qb[0:64, q0:q0 + QG], start=True, stop=True)
                    return e.matmul(SP[:, 512:1024], kb[64:128, j * P:(j + 1) * P],
                                    qb[64:128, q0:q0 + QG], start=True, stop=True)

                S.step("tensor", smm, reads=[qbk, kbk], writes=spk,
                       tag=f"C_s_{p}_{qg}_{j}")
                S.step("scalar", lambda e, SP=SP, ptb=ptb: e.activation(
                    ptb[:], SP[:], AF.Exp, scale=0.125),
                    reads=spk, writes=[ptk], tag=f"C_exp_{p}_{qg}_{j}")
                if j > 0:
                    mk_pv(j - 1)
            mk_pv(NT - 1)
            for hl in range(2):
                h = p * 2 + hl
                om = OH[hl]
                omk = OHK[hl]

                def onorm(e, om=om):
                    lview = om[:, 0:260].rearrange("p (c g) -> p c g", g=65)[:, :, 64:65]
                    S.bz(e, e.reciprocal(rq[:, 0:4], lview))
                    for c in range(4):
                        ins = e.tensor_scalar(ocst[:, c * 64:(c + 1) * 64],
                                              om[:, c * 65: c * 65 + 64],
                                              rq[:, c:c + 1], None, ALU.mult)
                    return ins

                S.step("vector", onorm, reads=[omk],
                       writes=[f"{pre}ocst", f"{pre}rq"], tag=f"C_on_{p}_{qg}_{hl}")
                S.dma("gpsimd",
                      OC_d[q0:q0 + QG, h * 64:(h + 1) * 64]
                      .rearrange("(c p) v -> p c v", p=P),
                      ocst[:].rearrange("p (c v) -> p c v", v=64),
                      reads=[f"{pre}ocst"], writes=[(f"{pre}ocd", h, qg)],
                      tag=f"C_ocst_{p}_{qg}_{hl}")

    all_out = []
    ock = [(f"{pre}ocd", h, g) for h in range(NH) for g in range(NQG)]
    for sp in range(NQG):
        r0 = sp * QG
        b_ = sp % 2
        xspb, otfb = xsp[b_], otf[b_]
        xk = [(f"{pre}xsp", b_, i) for i in range(4)]
        otk = [(f"{pre}otf", b_, c) for c in range(NF)]
        for i in range(4):
            S.dma("sync", xspb[i][:], aps["X"][r0 + i * P: r0 + (i + 1) * P, :],
                  writes=[xk[i]], tag=f"D_ldx_{sp}_{i}")
        for c in range(NF):
            S.dma("sync", otfb[c][:], OC_d[r0:r0 + QG, c * P:(c + 1) * P],
                  reads=ock, writes=[otk[c]], tag=f"D_ldot_{sp}_{c}", transpose=True)
        for i in range(4):
            S.step("vector", lambda e, i=i, xspb=xspb: e.tensor_tensor(
                xspb[i][:], xspb[i][:], bo_bc[:], ALU.add),
                reads=[xk[i]] + K_bobc, writes=[xk[i]], tag=f"D_xb_{sp}_{i}")
        for i in range(4):
            SP = SPa if i % 2 == 0 else SPb
            bk = "PS_SPa" if i % 2 == 0 else "PS_SPb"
            spk = [(bk, 0), (bk, 1)]

            def womm(e, i=i, SP=SP, otfb=otfb):
                last = None
                for eh in range(2):
                    for c in range(NF):
                        last = e.matmul(SP[:, eh * 512:eh * 512 + 512],
                                        otfb[c][:, i * P:(i + 1) * P],
                                        wo[:, c * D + eh * 512: c * D + eh * 512 + 512],
                                        start=(c == 0), stop=(c == NF - 1))
                return last

            S.step("tensor", womm, reads=otk + [f"{pre}WO"], writes=spk,
                   tag=f"D_wo_{sp}_{i}")
            S.step("vector", lambda e, i=i, SP=SP, xspb=xspb: e.tensor_tensor(
                xspb[i][:], xspb[i][:], SP[:], ALU.add),
                reads=spk + [xk[i]], writes=[xk[i]], tag=f"D_acc_{sp}_{i}")
        okeys = [((f"{pre}xo", sp, i), (f"{pre}xbo", sp, i)) for i in range(4)]
        all_out += [k for pair_ in okeys for k in pair_]
        emit_ln(S, nc, E, f"{pre}{b_}", xspb, xk, 4,
                aps["X_OUT"][r0:r0 + QG, :], aps["XBF_OUT"][r0:r0 + QG, :],
                okeys, M[3], "PS_M3", lnstat[b_])
    return all_out


L0ATTN_SPECS = [("X", (T, D), F32), ("XBF", (T, D), BF16),
                ("WQ", (D, D), F32), ("WK", (D, D), F32), ("WV", (D, D), F32),
                ("WO", (D, D), F32), ("BO", (D,), F32),
                ("CONESROW", (1, P), F32), ("CIDENT", (P, P), F32)]


def build_l0attn_program(debug=False):
    nc = bass.Bass("TRN2", target_bir_lowering=False)
    stack = ExitStack()
    aps = {}
    for name, shape, dt in L0ATTN_SPECS:
        aps[name] = nc.dram_tensor(name, list(shape), dt, kind="ExternalInput").ap()
    aps["X_OUT"] = nc.dram_tensor("X_OUT", [T, D], F32, kind="ExternalOutput").ap()
    aps["XBF_OUT"] = nc.dram_tensor("XBF_OUT", [T, D], BF16, kind="ExternalOutput").ap()
    with stack:
        E = stack.enter_context
        S = Sched(nc, E)
        ps = mk_psum_pool(nc, E)
        outk = build_l0attn(S, nc, E, aps, ps)
        S.step("sync", lambda e: e.nop(), reads=list(outk), tag="fence")
        S.emit()
    return nc


def build_l1attn(S, nc, E, aps, ps, pre="l1a", dbg=None):
    NQG = 8
    QG = 512
    E3 = 3 * D

    ident = E(nc.sbuf_tensor(f"{pre}_ident", [P, P], F32))
    qt_sb = [E(nc.sbuf_tensor(f"{pre}_qt{i}", [P, T], BF16)) for i in range(2)]
    kt_sb = [E(nc.sbuf_tensor(f"{pre}_kt{i}", [P, T], BF16)) for i in range(2)]
    v_sb = [E(nc.sbuf_tensor(f"{pre}_v{i}", [P, NT * 160], BF16)) for i in range(2)]
    pt = [E(nc.sbuf_tensor(f"{pre}_pt{i}", [P, 1024], BF16)) for i in range(2)]
    rq = E(nc.sbuf_tensor(f"{pre}_rq", [P, 4], F32))
    ocst = E(nc.sbuf_tensor(f"{pre}_ocst", [P, 256], F32))
    xsp1 = [E(nc.sbuf_tensor(f"{pre}_xsp_{i}", [P, D], F32)) for i in range(4)]
    xsp = [xsp1, xsp1]
    ocl = [E(nc.sbuf_tensor(f"{pre}_ocl{i}", [P, D], F32)) for i in range(4)]
    lnstat = [E(nc.sbuf_tensor(f"{pre}_lnstat{b}", [P, 48], F32)) for b in range(2)]
    inner = ExitStack()
    EB = inner.enter_context
    wT = [EB(nc.sbuf_tensor(f"{pre}_wT{c}", [P, E3], BF16)) for c in range(NF)]
    bq_sb = EB(nc.sbuf_tensor(f"{pre}_bq", [P, 24], F32))
    b24 = EB(nc.sbuf_tensor(f"{pre}_b24", [24, P], F32))
    b3row = EB(nc.sbuf_tensor(f"{pre}_b3row", [1, E3], F32))
    vb_bc = EB(nc.sbuf_tensor(f"{pre}_vbbc", [P, D], F32))
    onesr = EB(nc.sbuf_tensor(f"{pre}_ones", [1, P], F32))
    ropc = EB(nc.sbuf_tensor(f"{pre}_ropc", [P, T], BF16))
    rops = EB(nc.sbuf_tensor(f"{pre}_rops", [P, T], BF16))
    perm = EB(nc.sbuf_tensor(f"{pre}_perm", [P, P], BF16))
    xTs2 = [[EB(nc.sbuf_tensor(f"{pre}_xT{b}_{c}", [P, QG], BF16)) for c in range(NF)]
            for b in range(2)]
    stg = [[EB(nc.sbuf_tensor(f"{pre}_stg{i}_{b}", [P, QG], BF16)) for b in range(2)]
           for i in range(2)]
    rt1 = [EB(nc.sbuf_tensor(f"{pre}_rt1{b}", [P, QG], BF16)) for b in range(2)]
    rt2 = [EB(nc.sbuf_tensor(f"{pre}_rt2{b}", [P, QG], BF16)) for b in range(2)]
    v_st = [EB(nc.sbuf_tensor(f"{pre}_vst{i}", [P, P], BF16)) for i in range(2)]

    SPa, SPb = ps["SPa"], ps["SPb"]
    M = [ps[f"M{i}"] for i in range(4)]

    WBF_d = (aps["WQKVB"] if "WQKVB" in aps else
             nc.dram_tensor(f"{pre}_wbfd", [E3, D], BF16, kind="Internal").ap())
    QT_d = nc.dram_tensor(f"{pre}_qtd", [NPAIR * P, T], BF16, kind="Internal").ap()
    KT_d = nc.dram_tensor(f"{pre}_ktd", [NPAIR * P, T], BF16, kind="Internal").ap()
    V_d = nc.dram_tensor(f"{pre}_vd", [T, D], BF16, kind="Internal").ap()
    OC_d = nc.dram_tensor(f"{pre}_ocd", [T, D], F32, kind="Internal").ap()

    if "WQKVB" not in aps:
        S.dma("gpsimd", WBF_d, aps["WQKV"], writes=[f"{pre}wbfd"], tag="ldwq_cast")
        wbfk = [f"{pre}wbfd"]
    else:
        wbfk = []
    for c in range(NF):
        S.dma("sync", wT[c][:], WBF_d[:, c * P:(c + 1) * P], reads=wbfk,
              writes=[(f"{pre}wT", c)], tag=f"ldwT{c}", transpose=True)
    wTk = [(f"{pre}wT", c) for c in range(NF)]
    S.dma("sync", b24[:], aps["BQKV"].rearrange("(c p) -> c p", p=P),
          writes=[f"{pre}b3r"])
    S.dma("sync", b3row[:], aps["BQKV"][None, :], writes=[f"{pre}b3row"])
    S.dma("sync", onesr[:], aps["CONESROW"], writes=[f"{pre}ones"])
    S.dma("sync", ident[:], aps["CIDENT"], writes=[f"{pre}ident"])
    S.dma("sync", ropc[:], aps["CROPE_C"], writes=[f"{pre}ropc"])
    S.dma("sync", rops[:], aps["CROPE_S"], writes=[f"{pre}rops"])
    S.dma("gpsimd", perm[:], aps["CROPE_PERM"], writes=[f"{pre}perm"])

    S.step("tensor", lambda e: e.transpose(M[0][:, 0:24], b24[:], ident[0:24, 0:24]),
           reads=[f"{pre}b3r", f"{pre}ident"], writes=["PS_M0"], tag="btr")
    S.step("vector", lambda e: e.tensor_copy(bq_sb[:, 0:24], M[0][:, 0:24]),
           reads=["PS_M0"], writes=[f"{pre}bq"], tag="btr_ev")
    for p_ in range(NPAIR):
        S.step("tensor", lambda e, p_=p_: e.matmul(
            M[1][:, 0:P], onesr[:], b3row[0:1, 2 * D + p_ * P: 2 * D + (p_ + 1) * P],
            start=True, stop=True),
            reads=[f"{pre}b3row", f"{pre}ones"], writes=["PS_M1"], tag=f"vbb_{p_}")
        S.step("vector", lambda e, p_=p_: e.tensor_copy(
            vb_bc[:, p_ * P:(p_ + 1) * P], M[1][:, 0:P]),
            reads=["PS_M1"], writes=[(f"{pre}vbbc", p_)], tag=f"vbbe_{p_}")

    for qg in range(NQG):
        r0 = qg * QG
        xTs = xTs2[qg % 2]
        xtk = [(f"{pre}xT", qg % 2, c) for c in range(NF)]
        for c in range(NF):
            S.dma("sync", xTs[c][:], aps["XBF"][r0:r0 + QG, c * P:(c + 1) * P],
                  writes=[xtk[c]], tag=f"B_xT_{qg}_{c}", transpose=True)
        for p in range(NPAIR):
            for which, ofc, dram, dk in (("q", p, QT_d, f"{pre}qtd"),
                                         ("k", NPAIR + p, KT_d, f"{pre}ktd")):
                base = SPa if which == "q" else SPb
                bk = "PS_SPa" if which == "q" else "PS_SPb"
                mm_ps = base[:, (p % 2) * 512:(p % 2) * 512 + 512]
                mk = (bk, p % 2)
                sg = stg[0 if which == "q" else 1][p % 2]
                sgk = (f"{pre}stg", 0 if which == "q" else 1, p % 2)

                def projmm(e, ofc=ofc, mm_ps=mm_ps, xTs=xTs):
                    last = None
                    for c in range(NF):
                        last = e.matmul(mm_ps, wT[c][:, ofc * P:(ofc + 1) * P],
                                        xTs[c][:], start=(c == 0), stop=(c == NF - 1))
                    return last

                S.step("tensor", projmm, reads=xtk + wTk, writes=[mk],
                       tag=f"B_{which}mm_{qg}_{p}")
                S.step("scalar", lambda e, sg=sg, mm_ps=mm_ps, ofc=ofc: e.activation(
                    sg[:], mm_ps, AF.Identity, bias=bq_sb[:, ofc:ofc + 1]),
                    reads=[mk, f"{pre}bq"], writes=[sgk], tag=f"B_{which}ev_{qg}_{p}")
                ri = (0 if which == "q" else 1) + 2 * (p % 2)
                rmk = f"PS_M{ri}"
                rm = M[ri]
                S.step("tensor", lambda e, sg=sg, rm=rm: e.matmul(
                    rm[:], perm[:], sg[:], start=True, stop=True),
                    reads=[sgk, f"{pre}perm"], writes=[rmk], tag=f"B_{which}rot_{qg}_{p}")

                rt1b, rt2b = rt1[p % 2], rt2[p % 2]

                def ropemix(e, sg=sg, rm=rm, r0=r0, rt1b=rt1b, rt2b=rt2b):
                    S.bz(e, e.tensor_tensor(rt1b[:], sg[:], ropc[:, r0:r0 + QG], ALU.mult))
                    S.bz(e, e.tensor_tensor(rt2b[:], rm[:], rops[:, r0:r0 + QG], ALU.mult))
                    return e.tensor_tensor(sg[:], rt1b[:], rt2b[:], ALU.add)

                S.step("vector", ropemix,
                       reads=[sgk, rmk, f"{pre}ropc", f"{pre}rops"],
                       writes=[sgk, (f"{pre}rt", p % 2)], tag=f"B_{which}mix_{qg}_{p}")
                S.dma("gpsimd", dram[p * P:(p + 1) * P, r0:r0 + QG], sg[:],
                      reads=[sgk], writes=[(dk, p, qg)], tag=f"B_{which}st_{qg}_{p}")
            for tci in range(QG // P):
                tc = qg * (QG // P) + tci
                vp = M[2] if tci % 2 == 0 else M[3]
                vk = "PS_M2" if tci % 2 == 0 else "PS_M3"

                def vmm(e, p=p, tci=tci, vp=vp, xTs=xTs):
                    last = None
                    for c in range(NF):
                        last = e.matmul(vp[:, 0:P], xTs[c][:, tci * P:(tci + 1) * P],
                                        wT[c][:, 2 * D + p * P: 2 * D + (p + 1) * P],
                                        start=(c == 0), stop=(c == NF - 1))
                    return last

                S.step("tensor", vmm, reads=xtk + wTk, writes=[vk],
                       tag=f"B_vmm_{qg}_{p}_{tci}")
                sg = v_st[tci % 2]
                sgk = (f"{pre}vst", tci % 2)
                S.step("vector", lambda e, sg=sg, vp=vp, p=p: e.tensor_tensor(
                    sg[:], vp[:, 0:P], vb_bc[:, p * P:(p + 1) * P], ALU.add),
                    reads=[vk, (f"{pre}vbbc", p)], writes=[sgk],
                    tag=f"B_vev_{qg}_{p}_{tci}")
                S.dma("gpsimd", V_d[tc * P:(tc + 1) * P, p * P:(p + 1) * P], sg[:],
                      reads=[sgk], writes=[(f"{pre}vd", p, tc)],
                      tag=f"B_vst_{qg}_{p}_{tci}")

    inner.close()
    for i in range(2):
        S.step("vector", lambda e, i=i: e.memset(
            v_sb[i][:].rearrange("p (c g) -> p c g", g=80)[:, :, 64:65], 1.0),
            writes=[(f"{pre}vones", i)], tag=f"C_vones_{i}")

    for p in range(NPAIR):
        qb, kb, vb = qt_sb[p % 2], kt_sb[p % 2], v_sb[p % 2]
        qbk, kbk, vbk = (f"{pre}qt", p % 2), (f"{pre}kt", p % 2), (f"{pre}v", p % 2)
        S.dma("sync", qb[:], QT_d[p * P:(p + 1) * P, :],
              reads=[(f"{pre}qtd", p, g) for g in range(NQG)], writes=[qbk],
              tag=f"C_ldq_{p}")
        S.dma("sync", kb[:], KT_d[p * P:(p + 1) * P, :],
              reads=[(f"{pre}ktd", p, g) for g in range(NQG)], writes=[kbk],
              tag=f"C_ldk_{p}")
        vreads = [(f"{pre}vd", p, tc) for tc in range(NT)] + [(f"{pre}vones", p % 2)]
        for hl in range(2):
            S.dma("sync", vb[:].rearrange("p (c g) -> p c g", g=80)[:, hl::2, 0:64],
                  V_d[:, p * P + hl * 64: p * P + hl * 64 + 64]
                  .rearrange("(c p) v -> p c v", p=P),
                  reads=vreads, writes=[(vbk, hl)], tag=f"C_ldv_{p}_{hl}")
        for qg in range(NQG):
            q0 = qg * QG
            OH = [M[0 + 2 * (qg % 2)], M[1 + 2 * (qg % 2)]]
            OHK = ["PS_M%d" % (0 + 2 * (qg % 2)), "PS_M%d" % (1 + 2 * (qg % 2))]

            def mk_pv(j):
                ptb = pt[j % 2]
                ptk = (f"{pre}pt", j % 2)

                def pv(e, j=j, ptb=ptb, vb=vb, OH=OH):
                    last = None
                    for hl in range(2):
                        for c in range(4):
                            last = e.matmul(
                                OH[hl][:, c * 65:(c + 1) * 65],
                                ptb[:, hl * 512 + c * P: hl * 512 + (c + 1) * P],
                                vb[:, j * 160 + hl * 80: j * 160 + hl * 80 + 65],
                                start=(j == 0 and c == 0),
                                stop=(j == NT - 1 and c == 3))
                    return last

                S.step("tensor", pv, reads=[ptk, (vbk, 0), (vbk, 1)],
                       writes=OHK, tag=f"C_pv_{p}_{qg}_{j}")

            for j in range(NT):
                SP = SPa if j % 2 == 0 else SPb
                bk = "PS_SPa" if j % 2 == 0 else "PS_SPb"
                spk = [(bk, 0), (bk, 1)]
                ptb = pt[j % 2]
                ptk = (f"{pre}pt", j % 2)

                def smm(e, SP=SP, j=j, qb=qb, kb=kb, q0=q0):
                    e.matmul(SP[:, 0:512], kb[0:64, j * P:(j + 1) * P],
                             qb[0:64, q0:q0 + QG], start=True, stop=True)
                    return e.matmul(SP[:, 512:1024], kb[64:128, j * P:(j + 1) * P],
                                    qb[64:128, q0:q0 + QG], start=True, stop=True)

                S.step("tensor", smm, reads=[qbk, kbk], writes=spk,
                       tag=f"C_s_{p}_{qg}_{j}")
                S.step("scalar", lambda e, SP=SP, ptb=ptb: e.activation(
                    ptb[:], SP[:], AF.Exp, scale=0.125),
                    reads=spk, writes=[ptk], tag=f"C_exp_{p}_{qg}_{j}")
                if j > 0:
                    mk_pv(j - 1)
            mk_pv(NT - 1)
            for hl in range(2):
                h = p * 2 + hl
                om = OH[hl]
                omk = OHK[hl]

                def onorm(e, om=om):
                    lview = om[:, 0:260].rearrange("p (c g) -> p c g", g=65)[:, :, 64:65]
                    S.bz(e, e.reciprocal(rq[:, 0:4], lview))
                    for c in range(4):
                        ins = e.tensor_scalar(ocst[:, c * 64:(c + 1) * 64],
                                              om[:, c * 65: c * 65 + 64],
                                              rq[:, c:c + 1], None, ALU.mult)
                    return ins

                S.step("vector", onorm, reads=[omk],
                       writes=[f"{pre}ocst", f"{pre}rq"], tag=f"C_on_{p}_{qg}_{hl}")
                S.dma("gpsimd",
                      OC_d[q0:q0 + QG, h * 64:(h + 1) * 64]
                      .rearrange("(c p) v -> p c v", p=P),
                      ocst[:].rearrange("p (c v) -> p c v", v=64),
                      reads=[f"{pre}ocst"], writes=[(f"{pre}ocd", h, qg)],
                      tag=f"C_ocst_{p}_{qg}_{hl}")

    all_out = []
    ock = [(f"{pre}ocd", h, g) for h in range(NH) for g in range(NQG)]
    for sp in range(NQG):
        r0 = sp * QG
        b_ = sp % 2
        xspb, oclb = xsp[0], ocl
        xk = [(f"{pre}xsp", 0, i) for i in range(4)]
        for i in range(4):
            S.dma("sync", xspb[i][:], aps["X"][r0 + i * P: r0 + (i + 1) * P, :],
                  writes=[xk[i]], tag=f"D_ldx_{sp}_{i}")
            S.dma("sync", oclb[i][:], OC_d[r0 + i * P: r0 + (i + 1) * P, :],
                  reads=ock, writes=[(f"{pre}ocl", i)], tag=f"D_ldoc_{sp}_{i}")
            S.step("vector", lambda e, i=i, xspb=xspb, oclb=oclb: e.tensor_tensor(
                xspb[i][:], xspb[i][:], oclb[i][:], ALU.add),
                reads=[xk[i], (f"{pre}ocl", i)], writes=[xk[i]], tag=f"D_acc_{sp}_{i}")
        okeys = [((f"{pre}xo", sp, i), (f"{pre}xbo", sp, i)) for i in range(4)]
        all_out += [k for pair_ in okeys for k in pair_]
        emit_ln(S, nc, E, f"{pre}{b_}", xspb, xk, 4,
                aps["X_OUT"][r0:r0 + QG, :], aps["XBF_OUT"][r0:r0 + QG, :],
                okeys, M[3], "PS_M3", lnstat[b_])
    if dbg is not None:
        qtk = [(f"{pre}qtd", p_, g) for p_ in range(NPAIR) for g in range(NQG)]
        vdk = [(f"{pre}vd", p_, tc) for p_ in range(NPAIR) for tc in range(NT)]
        S.dma("gpsimd", dbg["QTD"], QT_d, reads=qtk, writes=["dbgq"], tag="dbg_qt")
        ktk = [(f"{pre}ktd", p_, g) for p_ in range(NPAIR) for g in range(NQG)]
        S.dma("gpsimd", dbg["KTD"], KT_d, reads=ktk, writes=["dbgk"], tag="dbg_kt")
        S.dma("gpsimd", dbg["VD"], V_d, reads=vdk, writes=["dbgv"], tag="dbg_v")
        S.dma("gpsimd", dbg["OCD"], OC_d, reads=ock, writes=["dbgo"], tag="dbg_oc")
        S.step("vector", lambda e: e.tensor_copy(osb[:, 0:24], bq_sb[:, 0:24]),
               reads=[f"{pre}bq"], writes=["dbgbq_s"], tag="dbg_bqc")
        S.dma("gpsimd", dbg["BQS"], osb[:, 0:24], reads=["dbgbq_s"], writes=["dbgbq"],
              tag="dbg_bq")
    return all_out


L1ATTN_SPECS = [("X", (T, D), F32), ("XBF", (T, D), BF16),
                ("WQKV", (3 * D, D), F32), ("BQKV", (3 * D,), F32),
                ("CROPE_C", (P, T), "bf"), ("CROPE_S", (P, T), "bf"),
                ("CROPE_PERM", (P, P), "bf"),
                ("CONESROW", (1, P), F32), ("CIDENT", (P, P), F32)]


def build_l1attn_program(debug=False):
    nc = bass.Bass("TRN2", target_bir_lowering=False)
    stack = ExitStack()
    aps = {}
    for name, shape, dt in L1ATTN_SPECS:
        dt = BF16 if dt == "bf" else dt
        aps[name] = nc.dram_tensor(name, list(shape), dt, kind="ExternalInput").ap()
    aps["X_OUT"] = nc.dram_tensor("X_OUT", [T, D], F32, kind="ExternalOutput").ap()
    aps["XBF_OUT"] = nc.dram_tensor("XBF_OUT", [T, D], BF16, kind="ExternalOutput").ap()
    dbg = None
    if debug:
        dbg = {"QTD": nc.dram_tensor("QTD", [NPAIR * P, T], BF16, kind="ExternalOutput").ap(),
               "KTD": nc.dram_tensor("KTD", [NPAIR * P, T], BF16, kind="ExternalOutput").ap(),
               "VD": nc.dram_tensor("VD", [T, D], BF16, kind="ExternalOutput").ap(),
               "OCD": nc.dram_tensor("OCD", [T, D], F32, kind="ExternalOutput").ap(),
               "BQS": nc.dram_tensor("BQS", [P, 24], F32, kind="ExternalOutput").ap()}
    with stack:
        E = stack.enter_context
        S = Sched(nc, E)
        ps = mk_psum_pool(nc, E)
        outk = build_l1attn(S, nc, E, aps, ps, dbg=dbg)
        S.step("sync", lambda e: e.nop(), reads=list(outk), tag="fence")
        S.emit()
    return nc



def emit_l0ffn_mbuild(S, nc, E, aps, ps, M_d, pre="l0fm"):
    NHD, NR = 4, 64
    TPB = ps["M2"]
    MM = ps["M3"]
    rm8 = E(nc.sbuf_tensor(f"{pre}_rm8", [P, 16], F32))
    rup2 = E(nc.sbuf_tensor(f"{pre}_rup2", [P, 16], F32))
    rmT = E(nc.sbuf_tensor(f"{pre}_rmT", [16, P], BF16))
    rupT8 = E(nc.sbuf_tensor(f"{pre}_rupT8", [16, 512], BF16))
    mst = E(nc.sbuf_tensor(f"{pre}_mst", [P, 512], F32))
    zeroz = E(nc.sbuf_tensor(f"{pre}_zeroz", [P, 512], BF16))
    identm = E(nc.sbuf_tensor(f"{pre}_ident", [P, P], F32))
    S.dma("sync", identm[:], aps["CIDENT"], writes=[f"{pre}ident"])
    S.step("vector", lambda e: e.memset(zeroz[:], 0.0), writes=[f"{pre}zz"],
           tag=f"{pre}_zz")
    zk = []
    for ch in range(32):
        k = (f"{pre}mdz", ch)
        zk.append(k)
        S.dma("gpsimd", M_d[:, ch * 512:(ch + 1) * 512], zeroz[:],
              reads=[f"{pre}zz"], writes=[k], tag=f"{pre}_zero_{ch}")
    for h in range(NHD):
        for o in range(8):
            r0 = 8 * o
            S.dma("sync", rm8[:], aps["RM"][h, r0:r0 + 8].rearrange("r i j -> (r i) j"),
                  writes=[f"{pre}rm8"], tag=f"M_rm_{h}_{o}")
            S.step("tensor", lambda e: e.transpose(TPB[0:16, 0:P], rm8[:], identm[:]),
                   reads=[f"{pre}rm8", f"{pre}ident"], writes=["PS_M2"],
                   tag=f"M_rmt_{h}_{o}")
            S.step("vector", lambda e: e.tensor_copy(rmT[:], TPB[0:16, 0:P]),
                   reads=["PS_M2"], writes=[f"{pre}rmT"], tag=f"M_rmte_{h}_{o}")
            for pr in range(4):
                r = r0 + 2 * pr
                S.dma("sync", rup2[:],
                      aps["RUP"][h, r:r + 2].rearrange("r k j -> (r k) j"),
                      writes=[f"{pre}rup2"], tag=f"M_rup_{h}_{o}_{pr}")
                S.step("tensor", lambda e: e.transpose(TPB[0:16, 0:P], rup2[:], identm[:]),
                       reads=[f"{pre}rup2", f"{pre}ident"], writes=["PS_M2"],
                       tag=f"M_rupt_{h}_{o}_{pr}")
                S.step("vector", lambda e, pr=pr: e.tensor_copy(
                    rupT8[:, pr * P:(pr + 1) * P], TPB[0:16, 0:P]),
                    reads=["PS_M2"], writes=[(f"{pre}rupT8", pr)],
                    tag=f"M_rupte_{h}_{o}_{pr}")
            S.step("tensor", lambda e: e.matmul(MM[:, 0:512], rmT[:], rupT8[:],
                                                start=True, stop=True),
                   reads=[f"{pre}rmT"] + [(f"{pre}rupT8", pr_) for pr_ in range(4)],
                   writes=["PS_M3"], tag=f"M_mm_{h}_{o}")
            S.step("vector", lambda e: e.tensor_copy(mst[:], MM[:, 0:512]),
                   reads=["PS_M3"], writes=[f"{pre}mst"], tag=f"M_ev_{h}_{o}")
            for li in range(8):
                rr = r0 + li
                c = 32 * h + rr // 2
                q = rr % 2
                roff = 16 * ((2 * (c % 32)) % 8) + q * 16
                S.dma("gpsimd",
                      M_d[roff:roff + 16, c * P + q * 64: c * P + (q + 1) * 64],
                      mst[li * 16:(li + 1) * 16, li * 64:(li + 1) * 64],
                      reads=[f"{pre}mst", zk[(c * P) // 512]],
                      writes=[(f"{pre}md", h, o, li)],
                      tag=f"M_st_{h}_{o}_{li}")
    return [(f"{pre}md", h, o, li) for h in range(NHD) for o in range(8)
            for li in range(8)] + zk


def build_l0ffn(S, nc, E, aps, ps, pre="l0f", dbg=None):
    NHD = 4
    NR = 64
    SPAN = 512
    NSP = T // SPAN
    NZ = 128
    NSUP = 4
    FSUP = 32

    mz = E(nc.sbuf_tensor(f"{pre}_mz", [P, NZ * P], BF16))
    ident = E(nc.sbuf_tensor(f"{pre}_ident", [P, P], F32))
    xTs = [[E(nc.sbuf_tensor(f"{pre}_xT{b}_{c}", [P, SPAN], BF16)) for c in range(NF)]
           for b in range(2)]
    xsp = [[E(nc.sbuf_tensor(f"{pre}_xsp{b}_{i}", [P, D], F32)) for i in range(4)]
           for b in range(2)]
    hc = [E(nc.sbuf_tensor(f"{pre}_hc{i}", [P, SPAN], BF16)) for i in range(FSUP)]
    wts = [E(nc.sbuf_tensor(f"{pre}_wt{i}", [P, D], BF16)) for i in range(FSUP)]
    lnstat = [E(nc.sbuf_tensor(f"{pre}_lnstat{b}", [P, 48], F32)) for b in range(2)]

    SPa, SPb = ps["SPa"], ps["SPb"]
    M = [ps["M0"], ps["M1"]]
    TPB = ps["M2"]
    MM = ps["M3"]

    if not aps.get("MD_PREBUILT"):
        M_d = nc.dram_tensor(f"{pre}_md", [P, NZ * P], BF16, kind="Internal").ap()
    S.dma("sync", ident[:], aps["CIDENT"], writes=[f"{pre}ident"])
    if "WCPB" in aps:
        WCPB = aps["WCPB"]
        wcpbk = []
    else:
        WCPB = nc.dram_tensor(f"{pre}_wcpb", [D, 16 * D], BF16, kind="Internal").ap()
        for ch in range(8):
            S.dma("gpsimd", WCPB[ch * P:(ch + 1) * P, :], aps["WCP"][ch * P:(ch + 1) * P, :],
                  writes=[(f"{pre}wcpb", ch)], tag=f"wcp_cast_{ch}")
        wcpbk = [(f"{pre}wcpb", ch) for ch in range(8)]

    if aps.get("MD_PREBUILT"):
        M_d, mdk = aps["MD_PREBUILT"]
    else:
        mdk = emit_l0ffn_mbuild(S, nc, E, aps, ps, M_d, pre=pre + "m")

    S.dma("sync", mz[:], M_d, reads=mdk, writes=[f"{pre}mzld"], tag="mz_ld")

    all_out = []
    for sp in range(NSP):
        r0 = sp * SPAN
        sb_ = sp % 2
        xspb, xTsb = xsp[sb_], xTs[sb_]
        xk = [(f"{pre}xsp", sb_, i) for i in range(4)]
        xtk = [(f"{pre}xT", sb_, c) for c in range(NF)]
        for i in range(4):
            S.dma("sync", xspb[i][:], aps["X"][r0 + i * P: r0 + (i + 1) * P, :],
                  writes=[xk[i]], tag=f"F_ldx_{sp}_{i}")
        for c in range(NF):
            S.dma("sync", xTsb[c][:], aps["XBF"][r0:r0 + SPAN, c * P:(c + 1) * P],
                  writes=[xtk[c]], tag=f"F_ldxT_{sp}_{c}", transpose=True)
        for sup in range(NSUP):
            wtk = [(f"{pre}wt", i) for i in range(FSUP)]
            for i in range(FSUP):
                c = sup * FSUP + i
                S.dma("sync", wts[i][:], WCPB[:, c * P:(c + 1) * P],
                      reads=wcpbk, writes=[wtk[i]], tag=f"F_ldw_{sp}_{sup}_{i}",
                      transpose=True)
            for i in range(FSUP):
                c = sup * FSUP + i
                fc = (c % 32) // 4
                zp = M[i % 2]
                zk = "PS_M0" if i % 2 == 0 else "PS_M1"
                S.step("tensor", lambda e, c=c, fc=fc, zp=zp, xTsb=xTsb: e.matmul(
                    zp[:], mz[:, c * P:(c + 1) * P], xTsb[fc][:],
                    start=True, stop=True),
                    reads=[f"{pre}mzld", xtk[fc]],
                    writes=[zk], tag=f"F_z_{sp}_{sup}_{i}")
                S.step("scalar", lambda e, i=i, zp=zp: e.activation(
                    hc[i][:], zp[:], AF.Gelu),
                    reads=[zk], writes=[(f"{pre}hc", i)], tag=f"F_g_{sp}_{sup}_{i}")
            hck = [(f"{pre}hc", i) for i in range(FSUP)]
            QQ = FSUP // 4
            for tc in range(4):
                ps = SPa if tc % 2 == 0 else SPb
                bk2 = "PS_SPa" if tc % 2 == 0 else "PS_SPb"
                psk = [(bk2, 0), (bk2, 1)]
                for qr in range(4):
                    def cp(e, tc=tc, ps=ps, qr=qr):
                        last = None
                        for i in range(qr * QQ, qr * QQ + QQ):
                            for eh in range(2):
                                last = e.matmul(ps[:, eh * 512:eh * 512 + 512],
                                                hc[i][:, tc * P:(tc + 1) * P],
                                                wts[i][:, eh * 512:eh * 512 + 512],
                                                start=(i == 0), stop=(i == FSUP - 1))
                        return last

                    S.step("tensor", cp,
                           reads=hck[qr * QQ:qr * QQ + QQ] + wtk[qr * QQ:qr * QQ + QQ],
                           writes=psk, tag=f"F_cp_{sp}_{sup}_{tc}_{qr}")
                S.step("vector", lambda e, tc=tc, ps=ps, xspb=xspb: e.tensor_tensor(
                    xspb[tc][:], xspb[tc][:], ps[:], ALU.add),
                    reads=psk + [xk[tc]], writes=[xk[tc]],
                    tag=f"F_acc_{sp}_{sup}_{tc}")
        okeys = [((f"{pre}xo", sp, i), (f"{pre}xbo", sp, i)) for i in range(4)]
        all_out += [k for pair_ in okeys for k in pair_]
        emit_ln(S, nc, E, f"{pre}{sb_}", xspb, xk, 4,
                aps["X_OUT"][r0:r0 + SPAN, :], aps["XBF_OUT"][r0:r0 + SPAN, :],
                okeys, MM, "PS_M3", lnstat[sb_])
    return all_out


L0FFN_SPECS = [("X", (T, D), F32), ("XBF", (T, D), BF16),
               ("RM", (4, 64, 16, 16), F32), ("RUP", (4, 64, 64, 16), F32),
               ("WCP", (D, 16 * D), F32),
               ("CONESROW", (1, P), F32), ("CIDENT", (P, P), F32)]


def build_l0ffn_program(debug=False):
    nc = bass.Bass("TRN2", target_bir_lowering=False)
    stack = ExitStack()
    aps = {}
    for name, shape, dt in L0FFN_SPECS:
        aps[name] = nc.dram_tensor(name, list(shape), dt, kind="ExternalInput").ap()
    aps["X_OUT"] = nc.dram_tensor("X_OUT", [T, D], F32, kind="ExternalOutput").ap()
    aps["XBF_OUT"] = nc.dram_tensor("XBF_OUT", [T, D], BF16, kind="ExternalOutput").ap()
    with stack:
        E = stack.enter_context
        S = Sched(nc, E)
        ps = mk_psum_pool(nc, E)
        outk = build_l0ffn(S, nc, E, aps, ps)
        S.step("sync", lambda e: e.nop(), reads=list(outk), tag="fence")
        S.emit()
    return nc


def _mk_program(builder, specs, first=False):
    nc = bass.Bass("TRN2", target_bir_lowering=False)
    stack = ExitStack()
    aps = {}
    for name, shape, dt in specs:
        if name == "XBF" and first:
            continue
        aps[name] = nc.dram_tensor(name, list(shape), dt, kind="ExternalInput").ap()
    if first:
        aps["XBF"] = nc.dram_tensor("XBF", [T, D], BF16, kind="Internal").ap()
    aps["X_OUT"] = nc.dram_tensor("X_OUT", [T, D], F32, kind="ExternalOutput").ap()
    aps["XBF_OUT"] = nc.dram_tensor("XBF_OUT", [T, D], BF16, kind="ExternalOutput").ap()
    with stack:
        E = stack.enter_context
        S = Sched(nc, E)
        ps = mk_psum_pool(nc, E)
        if first:
            keys = []
            for ch in range(8):
                S.dma("gpsimd", aps["XBF"][ch * 512:(ch + 1) * 512, :],
                      aps["X"][ch * 512:(ch + 1) * 512, :],
                      writes=[("xbfin", ch)], tag=f"xbf_cast_{ch}")
                keys.append(("xbfin", ch))
            S.dram_deps = {"XBF": keys}
        outk = builder(S, nc, E, aps, ps)
        S.step("sync", lambda e: e.nop(), reads=list(outk), tag="fence")
        S.emit()
    return nc


_PROG_CACHE = {}


def _get_programs():
    if "p" not in _PROG_CACHE:
        _PROG_CACHE["p"] = [
            ("l0attn", _mk_program(build_l0attn, L0ATTN_SPECS, first=True)),
            ("l0ffn", _mk_program(build_l0ffn, L0FFN_SPECS)),
            ("l1attn", _mk_program(build_l1attn,
                                   [(n, s, (BF16 if d == "bf" else d))
                                    for n, s, d in L1ATTN_SPECS])),
            ("l1ffn", _mk_program(build_l1ffn, L1FFN_SPECS)),
        ]
    return _PROG_CACHE["p"]




def build_fused_program():
    nc = bass.Bass("TRN2", target_bir_lowering=False)
    ext = {}
    for name, shape, dt in [("X", (T, D), F32),
                            ("WQ", (D, D), F32), ("WK", (D, D), F32),
                            ("WV", (D, D), F32), ("WO", (D, D), F32),
                            ("BO", (D,), F32),
                            ("RM", (4, 64, 16, 16), F32), ("RUP", (4, 64, 64, 16), F32),
                            ("WCP", (D, 16 * D), F32),
                            ("WQKV", (3 * D, D), F32), ("BQKV", (3 * D,), F32),
                            ("KER", (D, D), F32), ("BIAS", (D,), F32),
                            ("Aw", (D, 16), F32), ("Bw", (16, D), F32),
                            ("CONESROW", (1, P), F32), ("CIDENT", (P, P), F32),
                            ("CROPE_C", (P, T), BF16), ("CROPE_S", (P, T), BF16),
                            ("CROPE_PERM", (P, P), BF16)]:
        ext[name] = nc.dram_tensor(name, list(shape), dt, kind="ExternalInput").ap()
    OUT = nc.dram_tensor("OUT", [T, D], F32, kind="ExternalOutput").ap()
    XBF0 = nc.dram_tensor("XBF0", [T, D], BF16, kind="Internal").ap()
    X1 = nc.dram_tensor("X1", [T, D], F32, kind="Internal").ap()
    X1B = nc.dram_tensor("X1B", [T, D], BF16, kind="Internal").ap()
    X2 = nc.dram_tensor("X2", [T, D], F32, kind="Internal").ap()
    X2B = nc.dram_tensor("X2B", [T, D], BF16, kind="Internal").ap()
    X3 = nc.dram_tensor("X3", [T, D], F32, kind="Internal").ap()
    X3B = nc.dram_tensor("X3B", [T, D], BF16, kind="Internal").ap()
    XBFJ = nc.dram_tensor("XBFJ", [T, D], BF16, kind="Internal").ap()

    with ExitStack() as stack:
        E = stack.enter_context
        S = Sched(nc, E)
        S.dram_deps = {}
        ps = mk_psum_pool(nc, E)

        WCPB = nc.dram_tensor("g_wcpb", [D, 16 * D], BF16, kind="Internal").ap()
        WQKVB = nc.dram_tensor("g_wqkvb", [3 * D, D], BF16, kind="Internal").ap()
        for ch in range(8):
            S.dma("gpsimd", WCPB[ch * P:(ch + 1) * P, :],
                  ext["WCP"][ch * P:(ch + 1) * P, :],
                  writes=[("gwcpb", ch)], tag=f"g_wcp_cast_{ch}")
        S.dma("gpsimd", WQKVB, ext["WQKV"], writes=["gwqkvb"], tag="g_wqkv_cast")
        S.dram_deps_pre = {"g_wcpb": [("gwcpb", ch) for ch in range(8)],
                           "g_wqkvb": ["gwqkvb"]}

        keys = []
        for ch in range(8):
            S.dma("gpsimd", XBF0[ch * 512:(ch + 1) * 512, :],
                  ext["X"][ch * 512:(ch + 1) * 512, :],
                  writes=[("xbfin", ch)], tag=f"xbf_cast_{ch}")
            keys.append(("xbfin", ch))
        S.dram_deps["XBF0"] = keys
        S.dram_deps.update(S.dram_deps_pre)

        def barrier(outk, tag):
            fk = ("FENCE", tag)
            S.step("sync", lambda e: e.nop(), reads=list(outk), writes=[fk],
                   tag=f"fence_{tag}")
            for en in ("tensor", "scalar", "vector", "gpsimd"):
                S.step(en, lambda e: e.nop(), reads=[fk], tag=f"bar_{tag}_{en}")

        blocks = [
            (build_l0attn, "l0a",
             {"X": ext["X"], "XBF": XBF0, "WQ": ext["WQ"], "WK": ext["WK"],
              "WV": ext["WV"], "WO": ext["WO"], "BO": ext["BO"],
              "CONESROW": ext["CONESROW"], "CIDENT": ext["CIDENT"],
              "X_OUT": X1, "XBF_OUT": X1B}, ("X1", "X1B")),
            (build_l0ffn, "l0f",
             {"X": X1, "XBF": X1B, "RM": ext["RM"], "RUP": ext["RUP"],
              "WCP": ext["WCP"], "WCPB": WCPB, "CONESROW": ext["CONESROW"],
              "CIDENT": ext["CIDENT"], "X_OUT": X2, "XBF_OUT": X2B},
             ("X2", "X2B")),
            (build_l1attn, "l1a",
             {"X": X2, "XBF": X2B, "WQKV": ext["WQKV"], "WQKVB": WQKVB,
              "BQKV": ext["BQKV"],
              "CROPE_C": ext["CROPE_C"], "CROPE_S": ext["CROPE_S"],
              "CROPE_PERM": ext["CROPE_PERM"], "CONESROW": ext["CONESROW"],
              "CIDENT": ext["CIDENT"], "X_OUT": X3, "XBF_OUT": X3B},
             ("X3", "X3B")),
            (build_l1ffn, "l1f",
             {"X": X3, "XBF": X3B, "KER": ext["KER"], "BIAS": ext["BIAS"],
              "Aw": ext["Aw"], "Bw": ext["Bw"], "CONESROW": ext["CONESROW"],
              "CIDENT": ext["CIDENT"], "X_OUT": OUT, "XBF_OUT": XBFJ},
             None),
        ]
        MD = nc.dram_tensor("g_md", [P, 128 * P], BF16, kind="Internal").ap()
        KQD = nc.dram_tensor("g_kqd", [P, 8 * D], BF16, kind="Internal").ap()
        for bi, (builder, pre, baps, outnames) in enumerate(blocks):
            with ExitStack() as bst:
                BE = bst.enter_context
                outk = builder(S, nc, BE, baps, ps, pre=pre)
                fence_extra = []
                if bi == 0:
                    mdk = emit_l0ffn_mbuild(S, nc, BE,
                                            {"RM": ext["RM"], "RUP": ext["RUP"],
                                             "CIDENT": ext["CIDENT"]},
                                            ps, MD, pre="gm")
                    blocks[1][2]["MD_PREBUILT"] = (MD, mdk)
                    fence_extra = mdk
                if bi == 2:
                    kqdk = emit_l1ffn_quant(S, nc, BE,
                                            {"KER": ext["KER"],
                                             "CONESROW": ext["CONESROW"],
                                             "CIDENT": ext["CIDENT"]},
                                            ps, KQD, pre="gq")
                    blocks[3][2]["KQ_PREBUILT"] = (KQD, kqdk)
                    fence_extra = kqdk
            if outnames is not None:
                S.dram_deps[outnames[0]] = list(outk)
                S.dram_deps[outnames[1]] = list(outk)
            barrier(list(outk) + fence_extra, f"b{bi}")
        S.emit()
    return nc


_FUSED_CACHE = {}


def kernel(L0_attn_x, L0_attn_Wq, L0_attn_Wk, L0_attn_Wv, L0_attn_Wo, L0_attn_bo,
           L0_ffn_rail_maker, L0_ffn_rail_up_proj, L0_ffn_w_cproj,
           L1_attn_W_qkv, L1_attn_b_qkv,
           L1_ffn_kernel, L1_ffn_bias, L1_ffn_a, L1_ffn_b):
    consts = host_constants()
    ncores = 8
    x = np.ascontiguousarray(np.asarray(L0_attn_x, dtype=np.float32))
    if "nc" not in _FUSED_CACHE:
        _FUSED_CACHE["nc"] = build_fused_program()
    nc = _FUSED_CACHE["nc"]
    common = {"WQ": np.asarray(L0_attn_Wq), "WK": np.asarray(L0_attn_Wk),
              "WV": np.asarray(L0_attn_Wv), "WO": np.asarray(L0_attn_Wo),
              "BO": np.asarray(L0_attn_bo), "RM": np.asarray(L0_ffn_rail_maker),
              "RUP": np.asarray(L0_ffn_rail_up_proj),
              "WCP": np.asarray(L0_ffn_w_cproj),
              "WQKV": np.asarray(L1_attn_W_qkv), "BQKV": np.asarray(L1_attn_b_qkv),
              "KER": np.asarray(L1_ffn_kernel), "BIAS": np.asarray(L1_ffn_bias),
              "Aw": np.asarray(L1_ffn_a), "Bw": np.asarray(L1_ffn_b),
              "CONESROW": consts["CONESROW"], "CIDENT": consts["CIDENT"],
              "CROPE_C": consts["CROPE_C"], "CROPE_S": consts["CROPE_S"],
              "CROPE_PERM": consts["CROPE_PERM"]}
    maps = [{"X": x[c], **common} for c in range(ncores)]
    res = run_bass_kernel_spmd(nc, maps, core_ids=list(range(ncores)))
    return np.stack([np.asarray(r["OUT"]) for r in res.results], axis=0)


def kernel_unfused(L0_attn_x, L0_attn_Wq, L0_attn_Wk, L0_attn_Wv, L0_attn_Wo, L0_attn_bo,
           L0_ffn_rail_maker, L0_ffn_rail_up_proj, L0_ffn_w_cproj,
           L1_attn_W_qkv, L1_attn_b_qkv,
           L1_ffn_kernel, L1_ffn_bias, L1_ffn_a, L1_ffn_b):
    consts = host_constants()
    ncores = 8
    x = np.ascontiguousarray(np.asarray(L0_attn_x, dtype=np.float32))
    progs = dict(_get_programs())

    common = {"CONESROW": consts["CONESROW"], "CIDENT": consts["CIDENT"]}
    maps = [{"X": x[c], "WQ": np.asarray(L0_attn_Wq), "WK": np.asarray(L0_attn_Wk),
             "WV": np.asarray(L0_attn_Wv), "WO": np.asarray(L0_attn_Wo),
             "BO": np.asarray(L0_attn_bo), **common} for c in range(ncores)]
    res = run_bass_kernel_spmd(progs["l0attn"], maps, core_ids=list(range(ncores)))
    x1 = [np.asarray(r["X_OUT"]) for r in res.results]
    x1b = [np.asarray(r["XBF_OUT"]) for r in res.results]
    maps = [{"X": x1[c], "XBF": x1b[c], "RM": np.asarray(L0_ffn_rail_maker),
             "RUP": np.asarray(L0_ffn_rail_up_proj),
             "WCP": np.asarray(L0_ffn_w_cproj), **common} for c in range(ncores)]
    res = run_bass_kernel_spmd(progs["l0ffn"], maps, core_ids=list(range(ncores)))
    x2 = [np.asarray(r["X_OUT"]) for r in res.results]
    x2b = [np.asarray(r["XBF_OUT"]) for r in res.results]
    maps = [{"X": x2[c], "XBF": x2b[c], "WQKV": np.asarray(L1_attn_W_qkv),
             "BQKV": np.asarray(L1_attn_b_qkv), "CROPE_C": consts["CROPE_C"],
             "CROPE_S": consts["CROPE_S"], "CROPE_PERM": consts["CROPE_PERM"],
             **common} for c in range(ncores)]
    res = run_bass_kernel_spmd(progs["l1attn"], maps, core_ids=list(range(ncores)))
    x3 = [np.asarray(r["X_OUT"]) for r in res.results]
    x3b = [np.asarray(r["XBF_OUT"]) for r in res.results]
    maps = [{"X": x3[c], "XBF": x3b[c], "KER": np.asarray(L1_ffn_kernel),
             "BIAS": np.asarray(L1_ffn_bias), "Aw": np.asarray(L1_ffn_a),
             "Bw": np.asarray(L1_ffn_b), **common} for c in range(ncores)]
    res = run_bass_kernel_spmd(progs["l1ffn"], maps, core_ids=list(range(ncores)))
    out = np.stack([np.asarray(r["X_OUT"]) for r in res.results], axis=0)
    return out


# version 20
# speedup vs baseline: 1.0244x; 1.0244x over previous
import numpy as np
import ml_dtypes
from contextlib import ExitStack

import concourse.bass as bass
import concourse.mybir as mybir
from concourse.bass_utils import run_bass_kernel_spmd

F32 = mybir.dt.float32
BF16 = mybir.dt.bfloat16
I32 = mybir.dt.int32
AF = mybir.ActivationFunctionType
ALU = mybir.AluOpType
AX = mybir.AxisListType

P = 128
T = 4096
D = 1024
NT = T // P
NF = D // P
NH = 16
HD = 64
NPAIR = 8
EPS = 1e-5
BF = ml_dtypes.bfloat16

ENGINES = ("tensor", "scalar", "vector", "gpsimd", "sync")


class Sched:
    def __init__(self, nc, E):
        self.nc = nc
        self.E = E
        self.steps = []
        self.res = {}

    def step(self, engine, fn, reads=(), writes=(), dma=False, tag=""):
        sid = len(self.steps)
        deps = {}
        for r in reads:
            e = self.res.get(r)
            if e and e["w"] is not None:
                deps[e["w"]] = "raw"
        for w in writes:
            e = self.res.get(w)
            if e:
                if e["w"] is not None and deps.get(e["w"]) != "raw":
                    deps[e["w"]] = "waw"
                for rd in e["r"]:
                    deps.setdefault(rd, "war")
        for r in reads:
            self.res.setdefault(r, {"w": None, "r": []})["r"].append(sid)
        for w in writes:
            self.res[w] = {"w": sid, "r": []}
        self.steps.append(dict(id=sid, eng=engine, fn=fn, deps=deps, dma=dma,
                               need_mark=dma, mark=None, tag=tag))
        return sid

    @staticmethod
    def _dep_needs_sync(ds, st, kind):
        if ds["dma"]:
            return True
        if ds["eng"] != st["eng"]:
            return True
        return kind in ("raw", "waw") and st["eng"] != "tensor"

    def emit(self, limit=None):
        import os
        if limit is None:
            limit = int(os.environ.get("K_STEP_LIMIT", "0")) or None
        steps = self.steps if limit is None else self.steps[:limit]
        for st in steps:
            for d, kind in st["deps"].items():
                ds = steps[d]
                if self._dep_needs_sync(ds, st, kind):
                    ds["need_mark"] = True
        NP = 8
        counts = {}
        dmak = {}
        for st in steps:
            if st["dma"]:
                q = st["eng"]
                k = dmak.get(q, 0)
                dmak[q] = k + 1
                key = ("dmap", q, k % NP)
                st["mark"] = (key, 16 * (k // NP + 1))
                st["throttle"] = (key, 16 * (k // NP))
                counts[key] = 16 * (k // NP + 1)
            elif st["need_mark"]:
                k = ("e", st["eng"])
                counts[k] = counts.get(k, 0) + 1
                st["mark"] = (k, counts[k])
        sems = {k: self.E(self.nc.semaphore("s_" + "_".join(str(x) for x in k))) for k in counts}
        self._bsem = {en: self.E(self.nc.semaphore(f"bz_{en}"))
                      for en in ("vector", "scalar", "gpsimd")}
        self._bcnt = {en: 0 for en in self._bsem}
        block = self.E(self.nc.Block())
        for eng_name in ENGINES:
            mine = [st for st in steps if st["eng"] == eng_name]
            if not mine:
                continue

            def body(e, mine=mine, eng_name=eng_name):
                waited = {}
                self._cur_eng = eng_name
                for st in mine:
                    for d in sorted(st["deps"]):
                        kind = st["deps"][d]
                        ds = steps[d]
                        if ds["mark"] is None:
                            continue
                        if not self._dep_needs_sync(ds, st, kind):
                            continue
                        key, v = ds["mark"]
                        if waited.get(key, 0) >= v:
                            continue
                        e.wait_ge(sems[key], v)
                        waited[key] = v
                    if st["dma"]:
                        tk, tv = st["throttle"]
                        if tv > 0 and waited.get(tk, 0) < tv:
                            e.wait_ge(sems[tk], tv)
                            waited[tk] = tv
                    instr = st["fn"](e)
                    if st["mark"] is not None:
                        key = st["mark"][0]
                        instr.then_inc(sems[key], 16 if st["dma"] else 1)

            getattr(block, eng_name)(body)

    def bz(self, e, instr):
        en = self._cur_eng
        self._bcnt[en] += 1
        instr.then_inc(self._bsem[en], 1)
        e.wait_ge(self._bsem[en], self._bcnt[en])
        return instr

    def dma(self, q, out, in_, reads=(), writes=(), tag="", **kw):
        extra = []
        dd = getattr(self, "dram_deps", None)
        if dd and getattr(in_, "tensor", None) is not None:
            nm = getattr(in_.tensor, "name", "")
            if nm in dd:
                extra = dd[nm]
        return self.step(q, lambda e: e.dma_start(out=out, in_=in_, **kw),
                         reads=list(reads) + extra, writes=writes, dma=True, tag=tag)


def host_constants():
    c = {}
    c["CIDENT"] = np.eye(P, dtype=np.float32)
    c["CONESROW"] = np.ones((1, P), dtype=np.float32)
    c["CIDENTB"] = np.eye(P, dtype=np.float32).astype(BF)
    d = HD
    freqs = (1.0 / (10000.0 ** (np.arange(0, d, 2, dtype=np.float32) / float(d)))).astype(np.float32)
    ang = np.arange(T, dtype=np.float32)[:, None] * freqs[None, :]
    C1 = np.cos(ang).astype(np.float32).T[np.repeat(np.arange(32), 2)]
    S1 = np.sin(ang).astype(np.float32).T[np.repeat(np.arange(32), 2)]
    c["CROPE_C"] = np.ascontiguousarray(np.vstack([C1, C1])).astype(BF)
    c["CROPE_S"] = np.ascontiguousarray(np.vstack([S1, S1])).astype(BF)
    perm = np.zeros((P, P), dtype=np.float32)
    for blk in range(2):
        o = blk * 64
        for i in range(32):
            perm[o + 2 * i + 1, o + 2 * i] = -1.0
            perm[o + 2 * i, o + 2 * i + 1] = 1.0
    c["CROPE_PERM"] = perm.astype(BF)
    return c


def mk_psum_pool(nc, E):
    return {"SPa": E(nc.psum_tensor("ps_spa", [P, 1024], F32)),
            "SPb": E(nc.psum_tensor("ps_spb", [P, 1024], F32)),
            "M0": E(nc.psum_tensor("ps_m0", [P, 512], F32)),
            "M1": E(nc.psum_tensor("ps_m1", [P, 512], F32)),
            "M2": E(nc.psum_tensor("ps_m2", [P, 512], F32)),
            "M3": E(nc.psum_tensor("ps_m3", [P, 512], F32))}


def emit_ln(S, nc, E, pre, x_tiles, x_keys, n_tiles, x_out, xbf_out, out_keys,
            sq_ps, sq_key, st):
    nt = n_tiles
    ki = f"{pre}_lnstat"
    for i in range(n_tiles):
        xt, xk = x_tiles[i], x_keys[i]
        S.step("vector", lambda e, xt=xt, st=st, i=i: e.reduce_sum(
            st[:, i:i + 1], xt[:], axis=AX.X),
            reads=[xk], writes=[(ki, "s1", i)], tag=f"{pre}_ln_s1_{i}")
        S.step("scalar", lambda e, xt=xt, st=st, i=i: e.activation(
            sq_ps[:, 0:512], xt[:, 0:512], AF.Square, accum_out=st[:, 8 + i:9 + i]),
            reads=[xk], writes=[sq_key, (ki, "s2a", i)], tag=f"{pre}_ln_sqa_{i}")
        S.step("scalar", lambda e, xt=xt, st=st, i=i: e.activation(
            sq_ps[:, 0:512], xt[:, 512:1024], AF.Square, accum_out=st[:, 16 + i:17 + i]),
            reads=[xk], writes=[sq_key, (ki, "s2b", i)], tag=f"{pre}_ln_sqb_{i}")

    def var_chain(e, st=st, nt=nt):
        S.bz(e, e.tensor_scalar(st[:, 24:24 + nt], st[:, 0:nt], float(1.0 / D), None, ALU.mult))
        S.bz(e, e.tensor_tensor(st[:, 32:32 + nt], st[:, 8:8 + nt], st[:, 16:16 + nt], ALU.add))
        S.bz(e, e.tensor_scalar(st[:, 32:32 + nt], st[:, 32:32 + nt], float(1.0 / D), None, ALU.mult))
        S.bz(e, e.tensor_tensor(st[:, 40:40 + nt], st[:, 24:24 + nt], st[:, 24:24 + nt], ALU.mult))
        S.bz(e, e.tensor_tensor(st[:, 32:32 + nt], st[:, 32:32 + nt], st[:, 40:40 + nt], ALU.subtract))
        return e.tensor_scalar(st[:, 32:32 + nt], st[:, 32:32 + nt], float(EPS), None, ALU.add)

    allsums = [(ki, t, i) for t in ("s1", "s2a", "s2b") for i in range(nt)]
    S.step("vector", var_chain, reads=allsums, writes=[(ki, "var")],
           tag=f"{pre}_ln_var")

    def rstd_chain(e, st=st, nt=nt):
        S.bz(e, e.activation(st[:, 40:40 + nt], st[:, 32:32 + nt], AF.Ln))
        return e.activation(st[:, 40:40 + nt], st[:, 40:40 + nt], AF.Exp, scale=float(-0.5))

    S.step("scalar", rstd_chain, reads=[(ki, "var")], writes=[(ki, "rstd")],
           tag=f"{pre}_ln_rstd")
    for i in range(n_tiles):
        xt, xk = x_tiles[i], x_keys[i]
        S.step("vector", lambda e, xt=xt, st=st, i=i: e.tensor_scalar(
            xt[:], xt[:], st[:, 24 + i:25 + i], st[:, 40 + i:41 + i],
            ALU.subtract, ALU.mult),
            reads=[(ki, "rstd"), (ki, "var"), xk], writes=[xk],
            tag=f"{pre}_ln_norm_{i}")
        S.dma("gpsimd", x_out[i * P:(i + 1) * P, :], xt[:],
              reads=[xk], writes=[out_keys[i][0]], tag=f"{pre}_st_x_{i}")
        S.dma("gpsimd", xbf_out[i * P:(i + 1) * P, :], xt[:],
              reads=[xk], writes=[out_keys[i][1]], tag=f"{pre}_st_xbf_{i}")


def build_l1ffn(S, nc, E, aps, ps, pre="l1f", dbg=None):
    SPAN = 1024
    NSP = T // SPAN
    TPS = SPAN // P

    ksb = E(nc.sbuf_tensor(f"{pre}_ksb", [P, 8 * D], F32))
    kq = E(nc.sbuf_tensor(f"{pre}_kq", [P, 8 * D], BF16))
    qtmp = E(nc.sbuf_tensor(f"{pre}_qtmp", [P, D], F32))
    qtmpi = E(nc.sbuf_tensor(f"{pre}_qtmpi", [P, D], I32))
    stat = E(nc.sbuf_tensor(f"{pre}_stat", [P, 16], F32))
    onesr = E(nc.sbuf_tensor(f"{pre}_ones", [1, P], F32))
    brow = E(nc.sbuf_tensor(f"{pre}_brow", [1, D], F32))
    bias_bc = E(nc.sbuf_tensor(f"{pre}_biasbc", [P, D], F32))
    abf = E(nc.sbuf_tensor(f"{pre}_abf", [P, NF * 16], BF16))
    bbf = E(nc.sbuf_tensor(f"{pre}_bbf", [16, D], BF16))
    xsp = [E(nc.sbuf_tensor(f"{pre}_xsp{i}", [P, D], F32)) for i in range(TPS)]
    xTs = [E(nc.sbuf_tensor(f"{pre}_xT{c}", [P, SPAN], BF16)) for c in range(NF)]
    t1bf = E(nc.sbuf_tensor(f"{pre}_t1", [16, SPAN], BF16))
    mm_ps = [ps["SPa"], ps["SPb"]]
    t1_ps = ps["M0"]
    tp_ps = ps["M1"]
    sq_ps = ps["M2"]
    ident = E(nc.sbuf_tensor(f"{pre}_ident", [P, P], F32))

    K_ksb, K_kq, K_stat, K_tp = f"{pre}ksb", f"{pre}kq", f"{pre}stat", "PS_M1"
    K_tmp = f"{pre}tmp"

    S.dma("sync", ksb[:].rearrange("p (c e) -> p c e", e=D),
          aps["KER"].rearrange("(c p) e -> p c e", p=P),
          writes=[K_ksb], tag="ld_ker")
    S.dma("sync", brow[:], aps["BIAS"][None, :], writes=[f"{pre}brow"])
    S.dma("sync", onesr[:], aps["CONESROW"], writes=[f"{pre}ones"])
    S.dma("sync", ident[:], aps["CIDENT"], writes=[f"{pre}ident"])
    S.dma("gpsimd", abf[:].rearrange("p (c r) -> p c r", r=16),
          aps["Aw"].rearrange("(c p) r -> p c r", p=P),
          writes=[f"{pre}abf"])
    S.dma("gpsimd", bbf[:], aps["Bw"], writes=[f"{pre}bbf"])

    def qmax(e):
        for c in range(NF):
            ins = e.reduce_max(stat[:, c:c + 1], ksb[:, c * D:(c + 1) * D], axis=AX.X,
                               apply_absolute_value=True)
            if c == NF - 1:
                S.bz(e, ins)
        return e.reduce_max(stat[:, 8:9], stat[:, 0:8], axis=AX.X)

    S.step("vector", qmax, reads=[K_ksb], writes=[K_stat], tag="qmax")
    S.step("tensor", lambda e: e.transpose(tp_ps[0:1, 0:P], stat[:, 8:9], ident[:]),
           reads=[K_stat, f"{pre}ident"], writes=[K_tp], tag="qmax_tp")

    def qscale(e):
        S.bz(e, e.tensor_copy(qtmp[0:1, 0:P], tp_ps[0:1, 0:P]))
        S.bz(e, e.reduce_max(stat[0:1, 9:10], qtmp[0:1, 0:P], axis=AX.X))
        e.tensor_scalar(stat[0:1, 10:11], stat[0:1, 9:10], float(1.0 / 7.0), None, ALU.mult)
        S.bz(e, e.reciprocal(stat[0:1, 11:12], stat[0:1, 9:10]))
        return e.tensor_scalar(stat[0:1, 11:12], stat[0:1, 11:12], float(7.0), None, ALU.mult)

    S.step("vector", qscale, reads=[K_tp], writes=[K_stat, K_tmp], tag="qscale")
    S.step("tensor", lambda e: e.matmul(tp_ps[:, 0:2], onesr[:], stat[0:1, 10:12],
                                        start=True, stop=True),
           reads=[K_stat, f"{pre}ones"], writes=[K_tp], tag="qscale_bc")
    S.step("vector", lambda e: e.tensor_copy(stat[:, 12:14], tp_ps[:, 0:2]),
           reads=[K_tp], writes=[K_stat], tag="qscale_ev")

    for c in range(NF):
        def quant(e, c=c):
            sl = ksb[:, c * D:(c + 1) * D]
            S.bz(e, e.tensor_scalar(qtmp[:], sl, stat[:, 13:14], None, ALU.mult))
            S.bz(e, e.tensor_copy(qtmpi[:], qtmp[:]))
            S.bz(e, e.tensor_copy(qtmp[:], qtmpi[:]))
            S.bz(e, e.tensor_scalar(qtmp[:], qtmp[:], float(-8.0), float(7.0),
                                    ALU.max, ALU.min))
            return e.tensor_scalar(kq[:, c * D:(c + 1) * D], qtmp[:], stat[:, 12:13],
                                   None, ALU.mult)

        S.step("vector", quant, reads=[K_ksb, K_stat], writes=[(K_kq, c), K_tmp],
               tag=f"quant_{c}")

    if dbg is not None:
        S.dma("gpsimd", dbg["KQ"], kq[:], reads=[(K_kq, c) for c in range(NF)],
              writes=["dbg1"], tag="dbg_kq")
        S.dma("gpsimd", dbg["STAT"], stat[:], reads=[K_stat], writes=["dbg2"], tag="dbg_stat")

    S.step("tensor", lambda e: e.matmul(tp_ps[:, 0:P], onesr[:], brow[0:1, 0:P],
                                        start=True, stop=True),
           reads=[f"{pre}brow", f"{pre}ones"], writes=[K_tp], tag="bb0")
    S.step("vector", lambda e: e.tensor_copy(bias_bc[:, 0:P], tp_ps[:, 0:P]),
           reads=[K_tp], writes=[(f"{pre}bbc", 0)], tag="bb0e")
    for part in range(1, NF):
        S.step("tensor", lambda e, part=part: e.matmul(
            tp_ps[:, 0:P], onesr[:], brow[0:1, part * P:(part + 1) * P],
            start=True, stop=True), reads=[f"{pre}brow"], writes=[K_tp],
            tag=f"bb{part}")
        S.step("vector", lambda e, part=part: e.tensor_copy(
            bias_bc[:, part * P:(part + 1) * P], tp_ps[:, 0:P]),
            reads=[K_tp], writes=[(f"{pre}bbc", part)], tag=f"bb{part}e")
    K_bbc = [(f"{pre}bbc", i) for i in range(NF)]

    all_out = []
    lnstat = E(nc.sbuf_tensor(f"{pre}_lnstat", [P, 48], F32))
    for sp in range(NSP):
        r0 = sp * SPAN
        xk = [(f"{pre}xsp", i) for i in range(TPS)]
        xtk = [(f"{pre}xT", c) for c in range(NF)]
        for i in range(TPS):
            S.dma("sync", xsp[i][:], aps["X"][r0 + i * P: r0 + (i + 1) * P, :],
                  writes=[xk[i]], reads=[], tag=f"ldx_{sp}_{i}")
        for c in range(NF):
            S.dma("sync", xTs[c][:], aps["XBF"][r0:r0 + SPAN, c * P:(c + 1) * P],
                  writes=[xtk[c]], tag=f"ldxT_{sp}_{c}", transpose=True)
        for i in range(TPS):
            S.step("vector", lambda e, i=i: e.tensor_tensor(xsp[i][:], xsp[i][:],
                                                            bias_bc[:], ALU.add),
                   reads=[xk[i]] + K_bbc, writes=[xk[i]], tag=f"xb_{sp}_{i}")
        if dbg is not None and sp == 0:
            S.dma("gpsimd", dbg["XSP0"], xsp[0][:], reads=[xk[0]], writes=["dbg3"], tag="dbg_xsp")
            S.dma("gpsimd", dbg["XT0"], xTs[0][:], reads=[xtk[0]], writes=["dbg4"], tag="dbg_xt")
            S.dma("gpsimd", dbg["BBC"], bias_bc[:], reads=K_bbc, writes=["dbg5"], tag="dbg_bbc")
        for g in range(SPAN // 512):
            def lora1(e, g=g):
                last = None
                for c in range(NF):
                    last = e.matmul(t1_ps[0:16, :], abf[:, c * 16:(c + 1) * 16],
                                    xTs[c][:, g * 512:(g + 1) * 512],
                                    start=(c == 0), stop=(c == NF - 1))
                return last

            kt1ps = "PS_M0"
            S.step("tensor", lora1, reads=xtk + [f"{pre}abf"], writes=[kt1ps],
                   tag=f"lora1_{sp}_{g}")
            S.step("vector", lambda e, g=g: e.tensor_copy(
                t1bf[:, g * 512:(g + 1) * 512], t1_ps[0:16, :]),
                reads=[kt1ps], writes=[(f"{pre}t1bf", g)], tag=f"lora1e_{sp}_{g}")
        kt1 = [(f"{pre}t1bf", g) for g in range(SPAN // 512)]
        for i in range(TPS):
            ps = mm_ps[i % 2]
            kps_l = [("PS_SPa", 0), ("PS_SPa", 1)] if i % 2 == 0 else [("PS_SPb", 0), ("PS_SPb", 1)]

            def main_mm(e, i=i, ps=ps):
                last = None
                for eh in range(2):
                    for c in range(NF):
                        e.matmul(ps[:, eh * 512:eh * 512 + 512],
                                 xTs[c][:, i * P:(i + 1) * P],
                                 kq[:, c * D + eh * 512: c * D + eh * 512 + 512],
                                 start=(c == 0), stop=False)
                    last = e.matmul(ps[:, eh * 512:eh * 512 + 512],
                                    t1bf[:, i * P:(i + 1) * P],
                                    bbf[:, eh * 512:eh * 512 + 512],
                                    start=False, stop=True)
                return last

            S.step("tensor", main_mm,
                   reads=xtk + kt1 + [(K_kq, c) for c in range(NF)] + [f"{pre}bbf"],
                   writes=kps_l, tag=f"mm_{sp}_{i}")
            S.step("vector", lambda e, i=i, ps=ps: e.tensor_tensor(
                xsp[i][:], xsp[i][:], ps[:], ALU.add),
                reads=kps_l + [xk[i]], writes=[xk[i]], tag=f"acc_{sp}_{i}")
        if dbg is not None and sp == 0:
            S.dma("gpsimd", dbg["T1"], t1bf[:], reads=kt1, writes=["dbg6"], tag="dbg_t1")
            S.dma("gpsimd", dbg["PRELN"], xsp[0][:], reads=[xk[0]], writes=["dbg7"], tag="dbg_preln")
        okeys = [((f"{pre}xo", sp, i), (f"{pre}xbo", sp, i)) for i in range(TPS)]
        all_out += [k for pair in okeys for k in pair]
        emit_ln(S, nc, E, f"{pre}", xsp, xk, TPS,
                aps["X_OUT"][r0:r0 + SPAN, :], aps["XBF_OUT"][r0:r0 + SPAN, :],
                okeys, sq_ps, "PS_M2", lnstat)

    return all_out


def _build_block_program(block_fn, input_specs, extra_consts=True):
    nc = bass.Bass("TRN2", target_bir_lowering=False)
    stack = ExitStack()
    aps = {}
    for name, shape, dt in input_specs:
        aps[name] = nc.dram_tensor(name, list(shape), dt, kind="ExternalInput").ap()
    aps["X_OUT"] = nc.dram_tensor("X_OUT", [T, D], F32, kind="ExternalOutput").ap()
    aps["XBF_OUT"] = nc.dram_tensor("XBF_OUT", [T, D], BF16, kind="ExternalOutput").ap()
    with stack:
        E = stack.enter_context
        S = Sched(nc, E)
        outk = block_fn(S, nc, E, aps)
        S.step("sync", lambda e: e.nop(), reads=list(outk), tag="fence")
        S.emit()
    return nc


L1FFN_SPECS = [("X", (T, D), F32), ("XBF", (T, D), BF16),
               ("KER", (D, D), F32), ("BIAS", (D,), F32),
               ("Aw", (D, 16), F32), ("Bw", (16, D), F32),
               ("CONESROW", (1, P), F32), ("CIDENT", (P, P), F32)]


def build_l1ffn_program(debug=False):
    nc = bass.Bass("TRN2", target_bir_lowering=False)
    stack = ExitStack()
    aps = {}
    for name, shape, dt in L1FFN_SPECS:
        aps[name] = nc.dram_tensor(name, list(shape), dt, kind="ExternalInput").ap()
    aps["X_OUT"] = nc.dram_tensor("X_OUT", [T, D], F32, kind="ExternalOutput").ap()
    aps["XBF_OUT"] = nc.dram_tensor("XBF_OUT", [T, D], BF16, kind="ExternalOutput").ap()
    dbg = None
    if debug:
        dbg = {"KQ": nc.dram_tensor("KQ", [P, 8 * D], BF16, kind="ExternalOutput").ap(),
               "STAT": nc.dram_tensor("STAT", [P, 16], F32, kind="ExternalOutput").ap(),
               "XSP0": nc.dram_tensor("XSP0", [P, D], F32, kind="ExternalOutput").ap(),
               "XT0": nc.dram_tensor("XT0", [P, 1024], BF16, kind="ExternalOutput").ap(),
               "BBC": nc.dram_tensor("BBC", [P, D], F32, kind="ExternalOutput").ap(),
               "T1": nc.dram_tensor("T1", [16, 1024], BF16, kind="ExternalOutput").ap(),
               "PRELN": nc.dram_tensor("PRELN", [P, D], F32, kind="ExternalOutput").ap()}
    with stack:
        E = stack.enter_context
        S = Sched(nc, E)
        ps = mk_psum_pool(nc, E)
        outk = build_l1ffn(S, nc, E, aps, ps, dbg=dbg)
        S.step("sync", lambda e: e.nop(), reads=list(outk), tag="fence")
        S.emit()
    return nc




def build_l0attn(S, nc, E, aps, ps, pre="l0a", dbg=None):
    NQG = 8
    QG = 512

    wo = E(nc.sbuf_tensor(f"{pre}_wo", [P, NF * D], BF16))
    borow = E(nc.sbuf_tensor(f"{pre}_bo", [1, D], F32))
    bo_bc = E(nc.sbuf_tensor(f"{pre}_bobc", [P, D], F32))
    onesr = E(nc.sbuf_tensor(f"{pre}_ones", [1, P], F32))
    ident = E(nc.sbuf_tensor(f"{pre}_ident", [P, P], F32))
    qt_sb = [E(nc.sbuf_tensor(f"{pre}_qt{i}", [P, T], BF16)) for i in range(2)]
    kt_sb = [E(nc.sbuf_tensor(f"{pre}_kt{i}", [P, T], BF16)) for i in range(2)]
    v_sb = [E(nc.sbuf_tensor(f"{pre}_v{i}", [P, NT * 160], BF16)) for i in range(2)]
    pt = [E(nc.sbuf_tensor(f"{pre}_pt{i}", [P, 1024], BF16)) for i in range(2)]
    rq = E(nc.sbuf_tensor(f"{pre}_rq", [P, 4], F32))
    ocst = E(nc.sbuf_tensor(f"{pre}_ocst", [P, 256], BF16))
    xsp = [[E(nc.sbuf_tensor(f"{pre}_xsp{b}_{i}", [P, D], F32)) for i in range(4)]
           for b in range(2)]
    otf = [[E(nc.sbuf_tensor(f"{pre}_otf{b}_{c}", [P, QG], BF16)) for c in range(NF)]
           for b in range(2)]
    lnstat = [E(nc.sbuf_tensor(f"{pre}_lnstat{b}", [P, 48], F32)) for b in range(2)]
    inner = ExitStack()
    EB = inner.enter_context
    wq = EB(nc.sbuf_tensor(f"{pre}_wq", [P, NF * D], BF16))
    wk = EB(nc.sbuf_tensor(f"{pre}_wk", [P, NF * D], BF16))
    wv = EB(nc.sbuf_tensor(f"{pre}_wv", [P, NF * D], BF16))
    xTs2 = [[EB(nc.sbuf_tensor(f"{pre}_xT{b}_{c}", [P, QG], BF16)) for c in range(NF)]
            for b in range(2)]
    qt_st = [EB(nc.sbuf_tensor(f"{pre}_qtst{i}", [P, QG], BF16)) for i in range(2)]
    kt_st = [EB(nc.sbuf_tensor(f"{pre}_ktst{i}", [P, QG], BF16)) for i in range(2)]
    v_st = [EB(nc.sbuf_tensor(f"{pre}_vst{i}", [P, P], BF16)) for i in range(2)]

    SPa, SPb = ps["SPa"], ps["SPb"]
    M = [ps[f"M{i}"] for i in range(4)]

    QT_d = nc.dram_tensor(f"{pre}_qtd", [NPAIR * P, T], BF16, kind="Internal").ap()
    KT_d = nc.dram_tensor(f"{pre}_ktd", [NPAIR * P, T], BF16, kind="Internal").ap()
    V_d = nc.dram_tensor(f"{pre}_vd", [T, D], BF16, kind="Internal").ap()
    OC_d = nc.dram_tensor(f"{pre}_ocd", [T, D], BF16, kind="Internal").ap()

    for wbuf, nm in ((wq, "WQ"), (wk, "WK"), (wv, "WV"), (wo, "WO")):
        S.dma("gpsimd", wbuf[:].rearrange("p (c e) -> p c e", e=D),
              aps[nm].rearrange("(c p) e -> p c e", p=P), writes=[f"{pre}{nm}"],
              tag=f"ld{nm}")
    S.dma("sync", borow[:], aps["BO"][None, :], writes=[f"{pre}bo"])
    S.dma("sync", onesr[:], aps["CONESROW"], writes=[f"{pre}ones"])
    S.dma("sync", ident[:], aps["CIDENT"], writes=[f"{pre}ident"])

    for eh in range(2):
        S.step("tensor", lambda e, eh=eh: e.matmul(
            SPa[:, eh * 512:eh * 512 + 512], onesr[:], borow[0:1, eh * 512:eh * 512 + 512],
            start=True, stop=True), reads=[f"{pre}bo", f"{pre}ones"],
            writes=[("PS_SPa", eh)], tag=f"bobc{eh}")
        S.step("vector", lambda e, eh=eh: e.tensor_copy(
            bo_bc[:, eh * 512:eh * 512 + 512], SPa[:, eh * 512:eh * 512 + 512]),
            reads=[("PS_SPa", eh)], writes=[(f"{pre}bobc", eh)], tag=f"bobce{eh}")
    K_bobc = [(f"{pre}bobc", i) for i in range(2)]

    for qg in range(NQG):
        r0 = qg * QG
        xTs = xTs2[qg % 2]
        xtk = [(f"{pre}xT", qg % 2, c) for c in range(NF)]
        for c in range(NF):
            S.dma("sync", xTs[c][:], aps["XBF"][r0:r0 + QG, c * P:(c + 1) * P],
                  writes=[xtk[c]], tag=f"B_xT_{qg}_{c}", transpose=True)
        for p in range(NPAIR):
            for (wbuf, wkey, stg, stgk, dram) in (
                    (wq, f"{pre}WQ", qt_st[p % 2], (f"{pre}qtst", p % 2), QT_d),
                    (wk, f"{pre}WK", kt_st[p % 2], (f"{pre}ktst", p % 2), KT_d)):
                base = SPa if dram is QT_d else SPb
                bk = "PS_SPa" if dram is QT_d else "PS_SPb"
                mm_ps = base[:, (p % 2) * 512:(p % 2) * 512 + 512]
                mk = (bk, p % 2)

                def projmm(e, wbuf=wbuf, p=p, mm_ps=mm_ps, xTs=xTs):
                    last = None
                    for c in range(NF):
                        last = e.matmul(mm_ps, wbuf[:, c * D + p * P: c * D + (p + 1) * P],
                                        xTs[c][:], start=(c == 0), stop=(c == NF - 1))
                    return last

                S.step("tensor", projmm, reads=xtk + [wkey], writes=[mk],
                       tag=f"B_mm_{qg}_{p}")
                S.step("vector", lambda e, stg=stg, mm_ps=mm_ps: e.tensor_copy(
                    stg[:], mm_ps), reads=[mk], writes=[stgk],
                    tag=f"B_ev_{qg}_{p}")
                S.dma("gpsimd", dram[p * P:(p + 1) * P, r0:r0 + QG], stg[:],
                      reads=[stgk], writes=[(f"{pre}qtd" if dram is QT_d else f"{pre}ktd", p, qg)],
                      tag=f"B_st_{qg}_{p}")
            for tci in range(QG // P):
                tc = qg * (QG // P) + tci
                vp = M[2 + tci % 2]
                vk = "PS_M2" if tci % 2 == 0 else "PS_M3"

                def vmm(e, p=p, tci=tci, vp=vp, xTs=xTs):
                    last = None
                    for c in range(NF):
                        last = e.matmul(vp[:, 0:P], xTs[c][:, tci * P:(tci + 1) * P],
                                        wv[:, c * D + p * P: c * D + (p + 1) * P],
                                        start=(c == 0), stop=(c == NF - 1))
                    return last

                S.step("tensor", vmm, reads=xtk + [f"{pre}WV"], writes=[vk],
                       tag=f"B_vmm_{qg}_{p}_{tci}")
                stg = v_st[tci % 2]
                stgk = (f"{pre}vst", tci % 2)
                S.step("vector", lambda e, stg=stg, vp=vp: e.tensor_copy(
                    stg[:], vp[:, 0:P]), reads=[vk], writes=[stgk],
                    tag=f"B_vev_{qg}_{p}_{tci}")
                S.dma("gpsimd", V_d[tc * P:(tc + 1) * P, p * P:(p + 1) * P], stg[:],
                      reads=[stgk], writes=[(f"{pre}vd", p, tc)],
                      tag=f"B_vst_{qg}_{p}_{tci}")

    inner.close()
    for i in range(2):
        S.step("vector", lambda e, i=i: e.memset(
            v_sb[i][:].rearrange("p (c g) -> p c g", g=80)[:, :, 64:65], 1.0),
            writes=[(f"{pre}vones", i)], tag=f"C_vones_{i}")

    for p in range(NPAIR):
        qb, kb, vb = qt_sb[p % 2], kt_sb[p % 2], v_sb[p % 2]
        qbk, kbk, vbk = (f"{pre}qt", p % 2), (f"{pre}kt", p % 2), (f"{pre}v", p % 2)
        S.dma("sync", qb[:], QT_d[p * P:(p + 1) * P, :],
              reads=[(f"{pre}qtd", p, g) for g in range(NQG)], writes=[qbk],
              tag=f"C_ldq_{p}")
        S.dma("sync", kb[:], KT_d[p * P:(p + 1) * P, :],
              reads=[(f"{pre}ktd", p, g) for g in range(NQG)], writes=[kbk],
              tag=f"C_ldk_{p}")
        vreads = [(f"{pre}vd", p, tc) for tc in range(NT)] + [(f"{pre}vones", p % 2)]
        for hl in range(2):
            S.dma("sync", vb[:].rearrange("p (c g) -> p c g", g=80)[:, hl::2, 0:64],
                  V_d[:, p * P + hl * 64: p * P + hl * 64 + 64]
                  .rearrange("(c p) v -> p c v", p=P),
                  reads=vreads, writes=[(vbk, hl)], tag=f"C_ldv_{p}_{hl}")
        for qg in range(NQG):
            q0 = qg * QG
            OH = [M[0 + 2 * (qg % 2)], M[1 + 2 * (qg % 2)]]
            OHK = ["PS_M%d" % (0 + 2 * (qg % 2)), "PS_M%d" % (1 + 2 * (qg % 2))]

            def mk_pv(j):
                ptb = pt[j % 2]
                ptk = (f"{pre}pt", j % 2)

                def pv(e, j=j, ptb=ptb, vb=vb, OH=OH):
                    last = None
                    for hl in range(2):
                        for c in range(4):
                            last = e.matmul(
                                OH[hl][:, c * 65:(c + 1) * 65],
                                ptb[:, hl * 512 + c * P: hl * 512 + (c + 1) * P],
                                vb[:, j * 160 + hl * 80: j * 160 + hl * 80 + 65],
                                start=(j == 0 and c == 0),
                                stop=(j == NT - 1 and c == 3))
                    return last

                S.step("tensor", pv, reads=[ptk, (vbk, 0), (vbk, 1)],
                       writes=OHK, tag=f"C_pv_{p}_{qg}_{j}")

            for j in range(NT):
                SP = SPa if j % 2 == 0 else SPb
                bk = "PS_SPa" if j % 2 == 0 else "PS_SPb"
                spk = [(bk, 0), (bk, 1)]
                ptb = pt[j % 2]
                ptk = (f"{pre}pt", j % 2)

                def smm(e, SP=SP, j=j, qb=qb, kb=kb, q0=q0):
                    e.matmul(SP[:, 0:512], kb[0:64, j * P:(j + 1) * P],
                             qb[0:64, q0:q0 + QG], start=True, stop=True)
                    return e.matmul(SP[:, 512:1024], kb[64:128, j * P:(j + 1) * P],
                                    qb[64:128, q0:q0 + QG], start=True, stop=True)

                S.step("tensor", smm, reads=[qbk, kbk], writes=spk,
                       tag=f"C_s_{p}_{qg}_{j}")
                S.step("scalar", lambda e, SP=SP, ptb=ptb: e.activation(
                    ptb[:], SP[:], AF.Exp, scale=0.125),
                    reads=spk, writes=[ptk], tag=f"C_exp_{p}_{qg}_{j}")
                if j > 0:
                    mk_pv(j - 1)
            mk_pv(NT - 1)
            for hl in range(2):
                h = p * 2 + hl
                om = OH[hl]
                omk = OHK[hl]

                def onorm(e, om=om):
                    lview = om[:, 0:260].rearrange("p (c g) -> p c g", g=65)[:, :, 64:65]
                    S.bz(e, e.reciprocal(rq[:, 0:4], lview))
                    for c in range(4):
                        ins = e.tensor_scalar(ocst[:, c * 64:(c + 1) * 64],
                                              om[:, c * 65: c * 65 + 64],
                                              rq[:, c:c + 1], None, ALU.mult)
                    return ins

                S.step("vector", onorm, reads=[omk],
                       writes=[f"{pre}ocst", f"{pre}rq"], tag=f"C_on_{p}_{qg}_{hl}")
                S.dma("gpsimd",
                      OC_d[q0:q0 + QG, h * 64:(h + 1) * 64]
                      .rearrange("(c p) v -> p c v", p=P),
                      ocst[:].rearrange("p (c v) -> p c v", v=64),
                      reads=[f"{pre}ocst"], writes=[(f"{pre}ocd", h, qg)],
                      tag=f"C_ocst_{p}_{qg}_{hl}")

    all_out = []
    ock = [(f"{pre}ocd", h, g) for h in range(NH) for g in range(NQG)]
    for sp in range(NQG):
        r0 = sp * QG
        b_ = sp % 2
        xspb, otfb = xsp[b_], otf[b_]
        xk = [(f"{pre}xsp", b_, i) for i in range(4)]
        otk = [(f"{pre}otf", b_, c) for c in range(NF)]
        for i in range(4):
            S.dma("sync", xspb[i][:], aps["X"][r0 + i * P: r0 + (i + 1) * P, :],
                  writes=[xk[i]], tag=f"D_ldx_{sp}_{i}")
        for c in range(NF):
            S.dma("sync", otfb[c][:], OC_d[r0:r0 + QG, c * P:(c + 1) * P],
                  reads=ock, writes=[otk[c]], tag=f"D_ldot_{sp}_{c}", transpose=True)
        for i in range(4):
            S.step("vector", lambda e, i=i, xspb=xspb: e.tensor_tensor(
                xspb[i][:], xspb[i][:], bo_bc[:], ALU.add),
                reads=[xk[i]] + K_bobc, writes=[xk[i]], tag=f"D_xb_{sp}_{i}")
        for i in range(4):
            SP = SPa if i % 2 == 0 else SPb
            bk = "PS_SPa" if i % 2 == 0 else "PS_SPb"
            spk = [(bk, 0), (bk, 1)]

            def womm(e, i=i, SP=SP, otfb=otfb):
                last = None
                for eh in range(2):
                    for c in range(NF):
                        last = e.matmul(SP[:, eh * 512:eh * 512 + 512],
                                        otfb[c][:, i * P:(i + 1) * P],
                                        wo[:, c * D + eh * 512: c * D + eh * 512 + 512],
                                        start=(c == 0), stop=(c == NF - 1))
                return last

            S.step("tensor", womm, reads=otk + [f"{pre}WO"], writes=spk,
                   tag=f"D_wo_{sp}_{i}")
            S.step("vector", lambda e, i=i, SP=SP, xspb=xspb: e.tensor_tensor(
                xspb[i][:], xspb[i][:], SP[:], ALU.add),
                reads=spk + [xk[i]], writes=[xk[i]], tag=f"D_acc_{sp}_{i}")
        okeys = [((f"{pre}xo", sp, i), (f"{pre}xbo", sp, i)) for i in range(4)]
        all_out += [k for pair_ in okeys for k in pair_]
        emit_ln(S, nc, E, f"{pre}{b_}", xspb, xk, 4,
                aps["X_OUT"][r0:r0 + QG, :], aps["XBF_OUT"][r0:r0 + QG, :],
                okeys, M[3], "PS_M3", lnstat[b_])
    return all_out


L0ATTN_SPECS = [("X", (T, D), F32), ("XBF", (T, D), BF16),
                ("WQ", (D, D), F32), ("WK", (D, D), F32), ("WV", (D, D), F32),
                ("WO", (D, D), F32), ("BO", (D,), F32),
                ("CONESROW", (1, P), F32), ("CIDENT", (P, P), F32)]


def build_l0attn_program(debug=False):
    nc = bass.Bass("TRN2", target_bir_lowering=False)
    stack = ExitStack()
    aps = {}
    for name, shape, dt in L0ATTN_SPECS:
        aps[name] = nc.dram_tensor(name, list(shape), dt, kind="ExternalInput").ap()
    aps["X_OUT"] = nc.dram_tensor("X_OUT", [T, D], F32, kind="ExternalOutput").ap()
    aps["XBF_OUT"] = nc.dram_tensor("XBF_OUT", [T, D], BF16, kind="ExternalOutput").ap()
    with stack:
        E = stack.enter_context
        S = Sched(nc, E)
        ps = mk_psum_pool(nc, E)
        outk = build_l0attn(S, nc, E, aps, ps)
        S.step("sync", lambda e: e.nop(), reads=list(outk), tag="fence")
        S.emit()
    return nc


def build_l1attn(S, nc, E, aps, ps, pre="l1a", dbg=None):
    NQG = 8
    QG = 512
    E3 = 3 * D

    ident = E(nc.sbuf_tensor(f"{pre}_ident", [P, P], F32))
    qt_sb = [E(nc.sbuf_tensor(f"{pre}_qt{i}", [P, T], BF16)) for i in range(2)]
    kt_sb = [E(nc.sbuf_tensor(f"{pre}_kt{i}", [P, T], BF16)) for i in range(2)]
    v_sb = [E(nc.sbuf_tensor(f"{pre}_v{i}", [P, NT * 160], BF16)) for i in range(2)]
    pt = [E(nc.sbuf_tensor(f"{pre}_pt{i}", [P, 1024], BF16)) for i in range(2)]
    rq = E(nc.sbuf_tensor(f"{pre}_rq", [P, 4], F32))
    ocst = E(nc.sbuf_tensor(f"{pre}_ocst", [P, 256], F32))
    xsp = [[E(nc.sbuf_tensor(f"{pre}_xsp{b}_{i}", [P, D], F32)) for i in range(4)]
           for b in range(2)]
    ocl = [E(nc.sbuf_tensor(f"{pre}_ocl{i}", [P, D], F32)) for i in range(4)]
    lnstat = [E(nc.sbuf_tensor(f"{pre}_lnstat{b}", [P, 48], F32)) for b in range(2)]
    inner = ExitStack()
    EB = inner.enter_context
    wT = [EB(nc.sbuf_tensor(f"{pre}_wT{c}", [P, E3], BF16)) for c in range(NF)]
    bq_sb = EB(nc.sbuf_tensor(f"{pre}_bq", [P, 24], F32))
    b24 = EB(nc.sbuf_tensor(f"{pre}_b24", [24, P], F32))
    b3row = EB(nc.sbuf_tensor(f"{pre}_b3row", [1, E3], F32))
    vb_bc = EB(nc.sbuf_tensor(f"{pre}_vbbc", [P, D], F32))
    onesr = EB(nc.sbuf_tensor(f"{pre}_ones", [1, P], F32))
    ropc = EB(nc.sbuf_tensor(f"{pre}_ropc", [P, T], BF16))
    rops = EB(nc.sbuf_tensor(f"{pre}_rops", [P, T], BF16))
    perm = EB(nc.sbuf_tensor(f"{pre}_perm", [P, P], BF16))
    xTs2 = [[EB(nc.sbuf_tensor(f"{pre}_xT{b}_{c}", [P, QG], BF16)) for c in range(NF)]
            for b in range(2)]
    stg = [EB(nc.sbuf_tensor(f"{pre}_stg{i}", [P, QG], BF16)) for i in range(2)]
    rt1 = EB(nc.sbuf_tensor(f"{pre}_rt1", [P, QG], BF16))
    rt2 = EB(nc.sbuf_tensor(f"{pre}_rt2", [P, QG], BF16))
    v_st = [EB(nc.sbuf_tensor(f"{pre}_vst{i}", [P, P], BF16)) for i in range(2)]

    SPa, SPb = ps["SPa"], ps["SPb"]
    M = [ps[f"M{i}"] for i in range(4)]

    WBF_d = (aps["WQKVB"] if "WQKVB" in aps else
             nc.dram_tensor(f"{pre}_wbfd", [E3, D], BF16, kind="Internal").ap())
    QT_d = nc.dram_tensor(f"{pre}_qtd", [NPAIR * P, T], BF16, kind="Internal").ap()
    KT_d = nc.dram_tensor(f"{pre}_ktd", [NPAIR * P, T], BF16, kind="Internal").ap()
    V_d = nc.dram_tensor(f"{pre}_vd", [T, D], BF16, kind="Internal").ap()
    OC_d = nc.dram_tensor(f"{pre}_ocd", [T, D], F32, kind="Internal").ap()

    if "WQKVB" not in aps:
        S.dma("gpsimd", WBF_d, aps["WQKV"], writes=[f"{pre}wbfd"], tag="ldwq_cast")
        wbfk = [f"{pre}wbfd"]
    else:
        wbfk = []
    for c in range(NF):
        S.dma("sync", wT[c][:], WBF_d[:, c * P:(c + 1) * P], reads=wbfk,
              writes=[(f"{pre}wT", c)], tag=f"ldwT{c}", transpose=True)
    wTk = [(f"{pre}wT", c) for c in range(NF)]
    S.dma("sync", b24[:], aps["BQKV"].rearrange("(c p) -> c p", p=P),
          writes=[f"{pre}b3r"])
    S.dma("sync", b3row[:], aps["BQKV"][None, :], writes=[f"{pre}b3row"])
    S.dma("sync", onesr[:], aps["CONESROW"], writes=[f"{pre}ones"])
    S.dma("sync", ident[:], aps["CIDENT"], writes=[f"{pre}ident"])
    S.dma("sync", ropc[:], aps["CROPE_C"], writes=[f"{pre}ropc"])
    S.dma("sync", rops[:], aps["CROPE_S"], writes=[f"{pre}rops"])
    S.dma("gpsimd", perm[:], aps["CROPE_PERM"], writes=[f"{pre}perm"])

    S.step("tensor", lambda e: e.transpose(M[0][:, 0:24], b24[:], ident[0:24, 0:24]),
           reads=[f"{pre}b3r", f"{pre}ident"], writes=["PS_M0"], tag="btr")
    S.step("vector", lambda e: e.tensor_copy(bq_sb[:, 0:24], M[0][:, 0:24]),
           reads=["PS_M0"], writes=[f"{pre}bq"], tag="btr_ev")
    for p_ in range(NPAIR):
        S.step("tensor", lambda e, p_=p_: e.matmul(
            M[1][:, 0:P], onesr[:], b3row[0:1, 2 * D + p_ * P: 2 * D + (p_ + 1) * P],
            start=True, stop=True),
            reads=[f"{pre}b3row", f"{pre}ones"], writes=["PS_M1"], tag=f"vbb_{p_}")
        S.step("vector", lambda e, p_=p_: e.tensor_copy(
            vb_bc[:, p_ * P:(p_ + 1) * P], M[1][:, 0:P]),
            reads=["PS_M1"], writes=[(f"{pre}vbbc", p_)], tag=f"vbbe_{p_}")

    for qg in range(NQG):
        r0 = qg * QG
        xTs = xTs2[qg % 2]
        xtk = [(f"{pre}xT", qg % 2, c) for c in range(NF)]
        for c in range(NF):
            S.dma("sync", xTs[c][:], aps["XBF"][r0:r0 + QG, c * P:(c + 1) * P],
                  writes=[xtk[c]], tag=f"B_xT_{qg}_{c}", transpose=True)
        for p in range(NPAIR):
            for which, ofc, dram, dk in (("q", p, QT_d, f"{pre}qtd"),
                                         ("k", NPAIR + p, KT_d, f"{pre}ktd")):
                base = SPa if which == "q" else SPb
                bk = "PS_SPa" if which == "q" else "PS_SPb"
                mm_ps = base[:, (p % 2) * 512:(p % 2) * 512 + 512]
                mk = (bk, p % 2)
                sg = stg[0] if which == "q" else stg[1]
                sgk = (f"{pre}stg", 0 if which == "q" else 1)

                def projmm(e, ofc=ofc, mm_ps=mm_ps, xTs=xTs):
                    last = None
                    for c in range(NF):
                        last = e.matmul(mm_ps, wT[c][:, ofc * P:(ofc + 1) * P],
                                        xTs[c][:], start=(c == 0), stop=(c == NF - 1))
                    return last

                S.step("tensor", projmm, reads=xtk + wTk, writes=[mk],
                       tag=f"B_{which}mm_{qg}_{p}")
                S.step("scalar", lambda e, sg=sg, mm_ps=mm_ps, ofc=ofc: e.activation(
                    sg[:], mm_ps, AF.Identity, bias=bq_sb[:, ofc:ofc + 1]),
                    reads=[mk, f"{pre}bq"], writes=[sgk], tag=f"B_{which}ev_{qg}_{p}")
                rmk = "PS_M0" if which == "q" else "PS_M1"
                rm = M[0] if which == "q" else M[1]
                S.step("tensor", lambda e, sg=sg, rm=rm: e.matmul(
                    rm[:], perm[:], sg[:], start=True, stop=True),
                    reads=[sgk, f"{pre}perm"], writes=[rmk], tag=f"B_{which}rot_{qg}_{p}")

                def ropemix(e, sg=sg, rm=rm, r0=r0):
                    S.bz(e, e.tensor_tensor(rt1[:], sg[:], ropc[:, r0:r0 + QG], ALU.mult))
                    S.bz(e, e.tensor_tensor(rt2[:], rm[:], rops[:, r0:r0 + QG], ALU.mult))
                    return e.tensor_tensor(sg[:], rt1[:], rt2[:], ALU.add)

                S.step("vector", ropemix,
                       reads=[sgk, rmk, f"{pre}ropc", f"{pre}rops"],
                       writes=[sgk, f"{pre}rt"], tag=f"B_{which}mix_{qg}_{p}")
                S.dma("gpsimd", dram[p * P:(p + 1) * P, r0:r0 + QG], sg[:],
                      reads=[sgk], writes=[(dk, p, qg)], tag=f"B_{which}st_{qg}_{p}")
            for tci in range(QG // P):
                tc = qg * (QG // P) + tci
                vp = M[2] if tci % 2 == 0 else M[3]
                vk = "PS_M2" if tci % 2 == 0 else "PS_M3"

                def vmm(e, p=p, tci=tci, vp=vp, xTs=xTs):
                    last = None
                    for c in range(NF):
                        last = e.matmul(vp[:, 0:P], xTs[c][:, tci * P:(tci + 1) * P],
                                        wT[c][:, 2 * D + p * P: 2 * D + (p + 1) * P],
                                        start=(c == 0), stop=(c == NF - 1))
                    return last

                S.step("tensor", vmm, reads=xtk + wTk, writes=[vk],
                       tag=f"B_vmm_{qg}_{p}_{tci}")
                sg = v_st[tci % 2]
                sgk = (f"{pre}vst", tci % 2)
                S.step("vector", lambda e, sg=sg, vp=vp, p=p: e.tensor_tensor(
                    sg[:], vp[:, 0:P], vb_bc[:, p * P:(p + 1) * P], ALU.add),
                    reads=[vk, (f"{pre}vbbc", p)], writes=[sgk],
                    tag=f"B_vev_{qg}_{p}_{tci}")
                S.dma("gpsimd", V_d[tc * P:(tc + 1) * P, p * P:(p + 1) * P], sg[:],
                      reads=[sgk], writes=[(f"{pre}vd", p, tc)],
                      tag=f"B_vst_{qg}_{p}_{tci}")

    inner.close()
    for i in range(2):
        S.step("vector", lambda e, i=i: e.memset(
            v_sb[i][:].rearrange("p (c g) -> p c g", g=80)[:, :, 64:65], 1.0),
            writes=[(f"{pre}vones", i)], tag=f"C_vones_{i}")

    for p in range(NPAIR):
        qb, kb, vb = qt_sb[p % 2], kt_sb[p % 2], v_sb[p % 2]
        qbk, kbk, vbk = (f"{pre}qt", p % 2), (f"{pre}kt", p % 2), (f"{pre}v", p % 2)
        S.dma("sync", qb[:], QT_d[p * P:(p + 1) * P, :],
              reads=[(f"{pre}qtd", p, g) for g in range(NQG)], writes=[qbk],
              tag=f"C_ldq_{p}")
        S.dma("sync", kb[:], KT_d[p * P:(p + 1) * P, :],
              reads=[(f"{pre}ktd", p, g) for g in range(NQG)], writes=[kbk],
              tag=f"C_ldk_{p}")
        vreads = [(f"{pre}vd", p, tc) for tc in range(NT)] + [(f"{pre}vones", p % 2)]
        for hl in range(2):
            S.dma("sync", vb[:].rearrange("p (c g) -> p c g", g=80)[:, hl::2, 0:64],
                  V_d[:, p * P + hl * 64: p * P + hl * 64 + 64]
                  .rearrange("(c p) v -> p c v", p=P),
                  reads=vreads, writes=[(vbk, hl)], tag=f"C_ldv_{p}_{hl}")
        for qg in range(NQG):
            q0 = qg * QG
            OH = [M[0 + 2 * (qg % 2)], M[1 + 2 * (qg % 2)]]
            OHK = ["PS_M%d" % (0 + 2 * (qg % 2)), "PS_M%d" % (1 + 2 * (qg % 2))]

            def mk_pv(j):
                ptb = pt[j % 2]
                ptk = (f"{pre}pt", j % 2)

                def pv(e, j=j, ptb=ptb, vb=vb, OH=OH):
                    last = None
                    for hl in range(2):
                        for c in range(4):
                            last = e.matmul(
                                OH[hl][:, c * 65:(c + 1) * 65],
                                ptb[:, hl * 512 + c * P: hl * 512 + (c + 1) * P],
                                vb[:, j * 160 + hl * 80: j * 160 + hl * 80 + 65],
                                start=(j == 0 and c == 0),
                                stop=(j == NT - 1 and c == 3))
                    return last

                S.step("tensor", pv, reads=[ptk, (vbk, 0), (vbk, 1)],
                       writes=OHK, tag=f"C_pv_{p}_{qg}_{j}")

            for j in range(NT):
                SP = SPa if j % 2 == 0 else SPb
                bk = "PS_SPa" if j % 2 == 0 else "PS_SPb"
                spk = [(bk, 0), (bk, 1)]
                ptb = pt[j % 2]
                ptk = (f"{pre}pt", j % 2)

                def smm(e, SP=SP, j=j, qb=qb, kb=kb, q0=q0):
                    e.matmul(SP[:, 0:512], kb[0:64, j * P:(j + 1) * P],
                             qb[0:64, q0:q0 + QG], start=True, stop=True)
                    return e.matmul(SP[:, 512:1024], kb[64:128, j * P:(j + 1) * P],
                                    qb[64:128, q0:q0 + QG], start=True, stop=True)

                S.step("tensor", smm, reads=[qbk, kbk], writes=spk,
                       tag=f"C_s_{p}_{qg}_{j}")
                S.step("scalar", lambda e, SP=SP, ptb=ptb: e.activation(
                    ptb[:], SP[:], AF.Exp, scale=0.125),
                    reads=spk, writes=[ptk], tag=f"C_exp_{p}_{qg}_{j}")
                if j > 0:
                    mk_pv(j - 1)
            mk_pv(NT - 1)
            for hl in range(2):
                h = p * 2 + hl
                om = OH[hl]
                omk = OHK[hl]

                def onorm(e, om=om):
                    lview = om[:, 0:260].rearrange("p (c g) -> p c g", g=65)[:, :, 64:65]
                    S.bz(e, e.reciprocal(rq[:, 0:4], lview))
                    for c in range(4):
                        ins = e.tensor_scalar(ocst[:, c * 64:(c + 1) * 64],
                                              om[:, c * 65: c * 65 + 64],
                                              rq[:, c:c + 1], None, ALU.mult)
                    return ins

                S.step("vector", onorm, reads=[omk],
                       writes=[f"{pre}ocst", f"{pre}rq"], tag=f"C_on_{p}_{qg}_{hl}")
                S.dma("gpsimd",
                      OC_d[q0:q0 + QG, h * 64:(h + 1) * 64]
                      .rearrange("(c p) v -> p c v", p=P),
                      ocst[:].rearrange("p (c v) -> p c v", v=64),
                      reads=[f"{pre}ocst"], writes=[(f"{pre}ocd", h, qg)],
                      tag=f"C_ocst_{p}_{qg}_{hl}")

    all_out = []
    ock = [(f"{pre}ocd", h, g) for h in range(NH) for g in range(NQG)]
    for sp in range(NQG):
        r0 = sp * QG
        b_ = sp % 2
        xspb, oclb = xsp[b_], ocl
        xk = [(f"{pre}xsp", b_, i) for i in range(4)]
        for i in range(4):
            S.dma("sync", xspb[i][:], aps["X"][r0 + i * P: r0 + (i + 1) * P, :],
                  writes=[xk[i]], tag=f"D_ldx_{sp}_{i}")
            S.dma("sync", oclb[i][:], OC_d[r0 + i * P: r0 + (i + 1) * P, :],
                  reads=ock, writes=[(f"{pre}ocl", i)], tag=f"D_ldoc_{sp}_{i}")
            S.step("vector", lambda e, i=i, xspb=xspb, oclb=oclb: e.tensor_tensor(
                xspb[i][:], xspb[i][:], oclb[i][:], ALU.add),
                reads=[xk[i], (f"{pre}ocl", i)], writes=[xk[i]], tag=f"D_acc_{sp}_{i}")
        okeys = [((f"{pre}xo", sp, i), (f"{pre}xbo", sp, i)) for i in range(4)]
        all_out += [k for pair_ in okeys for k in pair_]
        emit_ln(S, nc, E, f"{pre}{b_}", xspb, xk, 4,
                aps["X_OUT"][r0:r0 + QG, :], aps["XBF_OUT"][r0:r0 + QG, :],
                okeys, M[3], "PS_M3", lnstat[b_])
    if dbg is not None:
        qtk = [(f"{pre}qtd", p_, g) for p_ in range(NPAIR) for g in range(NQG)]
        vdk = [(f"{pre}vd", p_, tc) for p_ in range(NPAIR) for tc in range(NT)]
        S.dma("gpsimd", dbg["QTD"], QT_d, reads=qtk, writes=["dbgq"], tag="dbg_qt")
        ktk = [(f"{pre}ktd", p_, g) for p_ in range(NPAIR) for g in range(NQG)]
        S.dma("gpsimd", dbg["KTD"], KT_d, reads=ktk, writes=["dbgk"], tag="dbg_kt")
        S.dma("gpsimd", dbg["VD"], V_d, reads=vdk, writes=["dbgv"], tag="dbg_v")
        S.dma("gpsimd", dbg["OCD"], OC_d, reads=ock, writes=["dbgo"], tag="dbg_oc")
        S.step("vector", lambda e: e.tensor_copy(osb[:, 0:24], bq_sb[:, 0:24]),
               reads=[f"{pre}bq"], writes=["dbgbq_s"], tag="dbg_bqc")
        S.dma("gpsimd", dbg["BQS"], osb[:, 0:24], reads=["dbgbq_s"], writes=["dbgbq"],
              tag="dbg_bq")
    return all_out


L1ATTN_SPECS = [("X", (T, D), F32), ("XBF", (T, D), BF16),
                ("WQKV", (3 * D, D), F32), ("BQKV", (3 * D,), F32),
                ("CROPE_C", (P, T), "bf"), ("CROPE_S", (P, T), "bf"),
                ("CROPE_PERM", (P, P), "bf"),
                ("CONESROW", (1, P), F32), ("CIDENT", (P, P), F32)]


def build_l1attn_program(debug=False):
    nc = bass.Bass("TRN2", target_bir_lowering=False)
    stack = ExitStack()
    aps = {}
    for name, shape, dt in L1ATTN_SPECS:
        dt = BF16 if dt == "bf" else dt
        aps[name] = nc.dram_tensor(name, list(shape), dt, kind="ExternalInput").ap()
    aps["X_OUT"] = nc.dram_tensor("X_OUT", [T, D], F32, kind="ExternalOutput").ap()
    aps["XBF_OUT"] = nc.dram_tensor("XBF_OUT", [T, D], BF16, kind="ExternalOutput").ap()
    dbg = None
    if debug:
        dbg = {"QTD": nc.dram_tensor("QTD", [NPAIR * P, T], BF16, kind="ExternalOutput").ap(),
               "KTD": nc.dram_tensor("KTD", [NPAIR * P, T], BF16, kind="ExternalOutput").ap(),
               "VD": nc.dram_tensor("VD", [T, D], BF16, kind="ExternalOutput").ap(),
               "OCD": nc.dram_tensor("OCD", [T, D], F32, kind="ExternalOutput").ap(),
               "BQS": nc.dram_tensor("BQS", [P, 24], F32, kind="ExternalOutput").ap()}
    with stack:
        E = stack.enter_context
        S = Sched(nc, E)
        ps = mk_psum_pool(nc, E)
        outk = build_l1attn(S, nc, E, aps, ps, dbg=dbg)
        S.step("sync", lambda e: e.nop(), reads=list(outk), tag="fence")
        S.emit()
    return nc



def emit_l0ffn_mbuild(S, nc, E, aps, ps, M_d, pre="l0fm"):
    NHD, NR = 4, 64
    TPB = ps["M2"]
    MM = ps["M3"]
    rm8 = E(nc.sbuf_tensor(f"{pre}_rm8", [P, 16], F32))
    rup2 = E(nc.sbuf_tensor(f"{pre}_rup2", [P, 16], F32))
    rmT = E(nc.sbuf_tensor(f"{pre}_rmT", [16, P], BF16))
    rupT8 = E(nc.sbuf_tensor(f"{pre}_rupT8", [16, 512], BF16))
    mst = E(nc.sbuf_tensor(f"{pre}_mst", [P, 512], F32))
    zeroz = E(nc.sbuf_tensor(f"{pre}_zeroz", [P, 512], BF16))
    identm = E(nc.sbuf_tensor(f"{pre}_ident", [P, P], F32))
    S.dma("sync", identm[:], aps["CIDENT"], writes=[f"{pre}ident"])
    S.step("vector", lambda e: e.memset(zeroz[:], 0.0), writes=[f"{pre}zz"],
           tag=f"{pre}_zz")
    zk = []
    for ch in range(32):
        k = (f"{pre}mdz", ch)
        zk.append(k)
        S.dma("gpsimd", M_d[:, ch * 512:(ch + 1) * 512], zeroz[:],
              reads=[f"{pre}zz"], writes=[k], tag=f"{pre}_zero_{ch}")
    for h in range(NHD):
        for o in range(8):
            r0 = 8 * o
            S.dma("sync", rm8[:], aps["RM"][h, r0:r0 + 8].rearrange("r i j -> (r i) j"),
                  writes=[f"{pre}rm8"], tag=f"M_rm_{h}_{o}")
            S.step("tensor", lambda e: e.transpose(TPB[0:16, 0:P], rm8[:], identm[:]),
                   reads=[f"{pre}rm8", f"{pre}ident"], writes=["PS_M2"],
                   tag=f"M_rmt_{h}_{o}")
            S.step("vector", lambda e: e.tensor_copy(rmT[:], TPB[0:16, 0:P]),
                   reads=["PS_M2"], writes=[f"{pre}rmT"], tag=f"M_rmte_{h}_{o}")
            for pr in range(4):
                r = r0 + 2 * pr
                S.dma("sync", rup2[:],
                      aps["RUP"][h, r:r + 2].rearrange("r k j -> (r k) j"),
                      writes=[f"{pre}rup2"], tag=f"M_rup_{h}_{o}_{pr}")
                S.step("tensor", lambda e: e.transpose(TPB[0:16, 0:P], rup2[:], identm[:]),
                       reads=[f"{pre}rup2", f"{pre}ident"], writes=["PS_M2"],
                       tag=f"M_rupt_{h}_{o}_{pr}")
                S.step("vector", lambda e, pr=pr: e.tensor_copy(
                    rupT8[:, pr * P:(pr + 1) * P], TPB[0:16, 0:P]),
                    reads=["PS_M2"], writes=[(f"{pre}rupT8", pr)],
                    tag=f"M_rupte_{h}_{o}_{pr}")
            S.step("tensor", lambda e: e.matmul(MM[:, 0:512], rmT[:], rupT8[:],
                                                start=True, stop=True),
                   reads=[f"{pre}rmT"] + [(f"{pre}rupT8", pr_) for pr_ in range(4)],
                   writes=["PS_M3"], tag=f"M_mm_{h}_{o}")
            S.step("vector", lambda e: e.tensor_copy(mst[:], MM[:, 0:512]),
                   reads=["PS_M3"], writes=[f"{pre}mst"], tag=f"M_ev_{h}_{o}")
            for li in range(8):
                rr = r0 + li
                c = 32 * h + rr // 2
                q = rr % 2
                roff = 16 * ((2 * (c % 32)) % 8) + q * 16
                S.dma("gpsimd",
                      M_d[roff:roff + 16, c * P + q * 64: c * P + (q + 1) * 64],
                      mst[li * 16:(li + 1) * 16, li * 64:(li + 1) * 64],
                      reads=[f"{pre}mst", zk[(c * P) // 512]],
                      writes=[(f"{pre}md", h, o, li)],
                      tag=f"M_st_{h}_{o}_{li}")
    return [(f"{pre}md", h, o, li) for h in range(NHD) for o in range(8)
            for li in range(8)] + zk


def build_l0ffn(S, nc, E, aps, ps, pre="l0f", dbg=None):
    NHD = 4
    NR = 64
    SPAN = 512
    NSP = T // SPAN
    NZ = 128
    NSUP = 4
    FSUP = 32

    mz = E(nc.sbuf_tensor(f"{pre}_mz", [P, NZ * P], BF16))
    ident = E(nc.sbuf_tensor(f"{pre}_ident", [P, P], F32))
    xTs = [[E(nc.sbuf_tensor(f"{pre}_xT{b}_{c}", [P, SPAN], BF16)) for c in range(NF)]
           for b in range(2)]
    xsp = [[E(nc.sbuf_tensor(f"{pre}_xsp{b}_{i}", [P, D], F32)) for i in range(4)]
           for b in range(2)]
    hc = [E(nc.sbuf_tensor(f"{pre}_hc{i}", [P, SPAN], BF16)) for i in range(FSUP)]
    wts = [E(nc.sbuf_tensor(f"{pre}_wt{i}", [P, D], BF16)) for i in range(FSUP)]
    lnstat = [E(nc.sbuf_tensor(f"{pre}_lnstat{b}", [P, 48], F32)) for b in range(2)]

    SPa, SPb = ps["SPa"], ps["SPb"]
    M = [ps["M0"], ps["M1"]]
    TPB = ps["M2"]
    MM = ps["M3"]

    if not aps.get("MD_PREBUILT"):
        M_d = nc.dram_tensor(f"{pre}_md", [P, NZ * P], BF16, kind="Internal").ap()
    S.dma("sync", ident[:], aps["CIDENT"], writes=[f"{pre}ident"])
    if "WCPB" in aps:
        WCPB = aps["WCPB"]
        wcpbk = []
    else:
        WCPB = nc.dram_tensor(f"{pre}_wcpb", [D, 16 * D], BF16, kind="Internal").ap()
        for ch in range(8):
            S.dma("gpsimd", WCPB[ch * P:(ch + 1) * P, :], aps["WCP"][ch * P:(ch + 1) * P, :],
                  writes=[(f"{pre}wcpb", ch)], tag=f"wcp_cast_{ch}")
        wcpbk = [(f"{pre}wcpb", ch) for ch in range(8)]

    if aps.get("MD_PREBUILT"):
        M_d, mdk = aps["MD_PREBUILT"]
    else:
        mdk = emit_l0ffn_mbuild(S, nc, E, aps, ps, M_d, pre=pre + "m")

    S.dma("sync", mz[:], M_d, reads=mdk, writes=[f"{pre}mzld"], tag="mz_ld")

    all_out = []
    for sp in range(NSP):
        r0 = sp * SPAN
        sb_ = sp % 2
        xspb, xTsb = xsp[sb_], xTs[sb_]
        xk = [(f"{pre}xsp", sb_, i) for i in range(4)]
        xtk = [(f"{pre}xT", sb_, c) for c in range(NF)]
        for i in range(4):
            S.dma("sync", xspb[i][:], aps["X"][r0 + i * P: r0 + (i + 1) * P, :],
                  writes=[xk[i]], tag=f"F_ldx_{sp}_{i}")
        for c in range(NF):
            S.dma("sync", xTsb[c][:], aps["XBF"][r0:r0 + SPAN, c * P:(c + 1) * P],
                  writes=[xtk[c]], tag=f"F_ldxT_{sp}_{c}", transpose=True)
        for sup in range(NSUP):
            wtk = [(f"{pre}wt", i) for i in range(FSUP)]
            for i in range(FSUP):
                c = sup * FSUP + i
                S.dma("sync", wts[i][:], WCPB[:, c * P:(c + 1) * P],
                      reads=wcpbk, writes=[wtk[i]], tag=f"F_ldw_{sp}_{sup}_{i}",
                      transpose=True)
            for i in range(FSUP):
                c = sup * FSUP + i
                fc = (c % 32) // 4
                zp = M[i % 2]
                zk = "PS_M0" if i % 2 == 0 else "PS_M1"
                S.step("tensor", lambda e, c=c, fc=fc, zp=zp, xTsb=xTsb: e.matmul(
                    zp[:], mz[:, c * P:(c + 1) * P], xTsb[fc][:],
                    start=True, stop=True),
                    reads=[f"{pre}mzld", xtk[fc]],
                    writes=[zk], tag=f"F_z_{sp}_{sup}_{i}")
                S.step("scalar", lambda e, i=i, zp=zp: e.activation(
                    hc[i][:], zp[:], AF.Gelu),
                    reads=[zk], writes=[(f"{pre}hc", i)], tag=f"F_g_{sp}_{sup}_{i}")
            hck = [(f"{pre}hc", i) for i in range(FSUP)]
            QQ = FSUP // 4
            for tc in range(4):
                ps = SPa if tc % 2 == 0 else SPb
                bk2 = "PS_SPa" if tc % 2 == 0 else "PS_SPb"
                psk = [(bk2, 0), (bk2, 1)]
                for qr in range(4):
                    def cp(e, tc=tc, ps=ps, qr=qr):
                        last = None
                        for i in range(qr * QQ, qr * QQ + QQ):
                            for eh in range(2):
                                last = e.matmul(ps[:, eh * 512:eh * 512 + 512],
                                                hc[i][:, tc * P:(tc + 1) * P],
                                                wts[i][:, eh * 512:eh * 512 + 512],
                                                start=(i == 0), stop=(i == FSUP - 1))
                        return last

                    S.step("tensor", cp,
                           reads=hck[qr * QQ:qr * QQ + QQ] + wtk[qr * QQ:qr * QQ + QQ],
                           writes=psk, tag=f"F_cp_{sp}_{sup}_{tc}_{qr}")
                S.step("vector", lambda e, tc=tc, ps=ps, xspb=xspb: e.tensor_tensor(
                    xspb[tc][:], xspb[tc][:], ps[:], ALU.add),
                    reads=psk + [xk[tc]], writes=[xk[tc]],
                    tag=f"F_acc_{sp}_{sup}_{tc}")
        okeys = [((f"{pre}xo", sp, i), (f"{pre}xbo", sp, i)) for i in range(4)]
        all_out += [k for pair_ in okeys for k in pair_]
        emit_ln(S, nc, E, f"{pre}{sb_}", xspb, xk, 4,
                aps["X_OUT"][r0:r0 + SPAN, :], aps["XBF_OUT"][r0:r0 + SPAN, :],
                okeys, MM, "PS_M3", lnstat[sb_])
    return all_out


L0FFN_SPECS = [("X", (T, D), F32), ("XBF", (T, D), BF16),
               ("RM", (4, 64, 16, 16), F32), ("RUP", (4, 64, 64, 16), F32),
               ("WCP", (D, 16 * D), F32),
               ("CONESROW", (1, P), F32), ("CIDENT", (P, P), F32)]


def build_l0ffn_program(debug=False):
    nc = bass.Bass("TRN2", target_bir_lowering=False)
    stack = ExitStack()
    aps = {}
    for name, shape, dt in L0FFN_SPECS:
        aps[name] = nc.dram_tensor(name, list(shape), dt, kind="ExternalInput").ap()
    aps["X_OUT"] = nc.dram_tensor("X_OUT", [T, D], F32, kind="ExternalOutput").ap()
    aps["XBF_OUT"] = nc.dram_tensor("XBF_OUT", [T, D], BF16, kind="ExternalOutput").ap()
    with stack:
        E = stack.enter_context
        S = Sched(nc, E)
        ps = mk_psum_pool(nc, E)
        outk = build_l0ffn(S, nc, E, aps, ps)
        S.step("sync", lambda e: e.nop(), reads=list(outk), tag="fence")
        S.emit()
    return nc


def _mk_program(builder, specs, first=False):
    nc = bass.Bass("TRN2", target_bir_lowering=False)
    stack = ExitStack()
    aps = {}
    for name, shape, dt in specs:
        if name == "XBF" and first:
            continue
        aps[name] = nc.dram_tensor(name, list(shape), dt, kind="ExternalInput").ap()
    if first:
        aps["XBF"] = nc.dram_tensor("XBF", [T, D], BF16, kind="Internal").ap()
    aps["X_OUT"] = nc.dram_tensor("X_OUT", [T, D], F32, kind="ExternalOutput").ap()
    aps["XBF_OUT"] = nc.dram_tensor("XBF_OUT", [T, D], BF16, kind="ExternalOutput").ap()
    with stack:
        E = stack.enter_context
        S = Sched(nc, E)
        ps = mk_psum_pool(nc, E)
        if first:
            keys = []
            for ch in range(8):
                S.dma("gpsimd", aps["XBF"][ch * 512:(ch + 1) * 512, :],
                      aps["X"][ch * 512:(ch + 1) * 512, :],
                      writes=[("xbfin", ch)], tag=f"xbf_cast_{ch}")
                keys.append(("xbfin", ch))
            S.dram_deps = {"XBF": keys}
        outk = builder(S, nc, E, aps, ps)
        S.step("sync", lambda e: e.nop(), reads=list(outk), tag="fence")
        S.emit()
    return nc


_PROG_CACHE = {}


def _get_programs():
    if "p" not in _PROG_CACHE:
        _PROG_CACHE["p"] = [
            ("l0attn", _mk_program(build_l0attn, L0ATTN_SPECS, first=True)),
            ("l0ffn", _mk_program(build_l0ffn, L0FFN_SPECS)),
            ("l1attn", _mk_program(build_l1attn,
                                   [(n, s, (BF16 if d == "bf" else d))
                                    for n, s, d in L1ATTN_SPECS])),
            ("l1ffn", _mk_program(build_l1ffn, L1FFN_SPECS)),
        ]
    return _PROG_CACHE["p"]




def build_fused_program():
    nc = bass.Bass("TRN2", target_bir_lowering=False)
    ext = {}
    for name, shape, dt in [("X", (T, D), F32),
                            ("WQ", (D, D), F32), ("WK", (D, D), F32),
                            ("WV", (D, D), F32), ("WO", (D, D), F32),
                            ("BO", (D,), F32),
                            ("RM", (4, 64, 16, 16), F32), ("RUP", (4, 64, 64, 16), F32),
                            ("WCP", (D, 16 * D), F32),
                            ("WQKV", (3 * D, D), F32), ("BQKV", (3 * D,), F32),
                            ("KER", (D, D), F32), ("BIAS", (D,), F32),
                            ("Aw", (D, 16), F32), ("Bw", (16, D), F32),
                            ("CONESROW", (1, P), F32), ("CIDENT", (P, P), F32),
                            ("CROPE_C", (P, T), BF16), ("CROPE_S", (P, T), BF16),
                            ("CROPE_PERM", (P, P), BF16)]:
        ext[name] = nc.dram_tensor(name, list(shape), dt, kind="ExternalInput").ap()
    OUT = nc.dram_tensor("OUT", [T, D], F32, kind="ExternalOutput").ap()
    XBF0 = nc.dram_tensor("XBF0", [T, D], BF16, kind="Internal").ap()
    X1 = nc.dram_tensor("X1", [T, D], F32, kind="Internal").ap()
    X1B = nc.dram_tensor("X1B", [T, D], BF16, kind="Internal").ap()
    X2 = nc.dram_tensor("X2", [T, D], F32, kind="Internal").ap()
    X2B = nc.dram_tensor("X2B", [T, D], BF16, kind="Internal").ap()
    X3 = nc.dram_tensor("X3", [T, D], F32, kind="Internal").ap()
    X3B = nc.dram_tensor("X3B", [T, D], BF16, kind="Internal").ap()
    XBFJ = nc.dram_tensor("XBFJ", [T, D], BF16, kind="Internal").ap()

    with ExitStack() as stack:
        E = stack.enter_context
        S = Sched(nc, E)
        S.dram_deps = {}
        ps = mk_psum_pool(nc, E)

        WCPB = nc.dram_tensor("g_wcpb", [D, 16 * D], BF16, kind="Internal").ap()
        WQKVB = nc.dram_tensor("g_wqkvb", [3 * D, D], BF16, kind="Internal").ap()
        for ch in range(8):
            S.dma("gpsimd", WCPB[ch * P:(ch + 1) * P, :],
                  ext["WCP"][ch * P:(ch + 1) * P, :],
                  writes=[("gwcpb", ch)], tag=f"g_wcp_cast_{ch}")
        S.dma("gpsimd", WQKVB, ext["WQKV"], writes=["gwqkvb"], tag="g_wqkv_cast")
        S.dram_deps_pre = {"g_wcpb": [("gwcpb", ch) for ch in range(8)],
                           "g_wqkvb": ["gwqkvb"]}

        keys = []
        for ch in range(8):
            S.dma("gpsimd", XBF0[ch * 512:(ch + 1) * 512, :],
                  ext["X"][ch * 512:(ch + 1) * 512, :],
                  writes=[("xbfin", ch)], tag=f"xbf_cast_{ch}")
            keys.append(("xbfin", ch))
        S.dram_deps["XBF0"] = keys
        S.dram_deps.update(S.dram_deps_pre)

        def barrier(outk, tag):
            fk = ("FENCE", tag)
            S.step("sync", lambda e: e.nop(), reads=list(outk), writes=[fk],
                   tag=f"fence_{tag}")
            for en in ("tensor", "scalar", "vector", "gpsimd"):
                S.step(en, lambda e: e.nop(), reads=[fk], tag=f"bar_{tag}_{en}")

        blocks = [
            (build_l0attn, "l0a",
             {"X": ext["X"], "XBF": XBF0, "WQ": ext["WQ"], "WK": ext["WK"],
              "WV": ext["WV"], "WO": ext["WO"], "BO": ext["BO"],
              "CONESROW": ext["CONESROW"], "CIDENT": ext["CIDENT"],
              "X_OUT": X1, "XBF_OUT": X1B}, ("X1", "X1B")),
            (build_l0ffn, "l0f",
             {"X": X1, "XBF": X1B, "RM": ext["RM"], "RUP": ext["RUP"],
              "WCP": ext["WCP"], "WCPB": WCPB, "CONESROW": ext["CONESROW"],
              "CIDENT": ext["CIDENT"], "X_OUT": X2, "XBF_OUT": X2B},
             ("X2", "X2B")),
            (build_l1attn, "l1a",
             {"X": X2, "XBF": X2B, "WQKV": ext["WQKV"], "WQKVB": WQKVB,
              "BQKV": ext["BQKV"],
              "CROPE_C": ext["CROPE_C"], "CROPE_S": ext["CROPE_S"],
              "CROPE_PERM": ext["CROPE_PERM"], "CONESROW": ext["CONESROW"],
              "CIDENT": ext["CIDENT"], "X_OUT": X3, "XBF_OUT": X3B},
             ("X3", "X3B")),
            (build_l1ffn, "l1f",
             {"X": X3, "XBF": X3B, "KER": ext["KER"], "BIAS": ext["BIAS"],
              "Aw": ext["Aw"], "Bw": ext["Bw"], "CONESROW": ext["CONESROW"],
              "CIDENT": ext["CIDENT"], "X_OUT": OUT, "XBF_OUT": XBFJ},
             None),
        ]
        MD = nc.dram_tensor("g_md", [P, 128 * P], BF16, kind="Internal").ap()
        for bi, (builder, pre, baps, outnames) in enumerate(blocks):
            with ExitStack() as bst:
                BE = bst.enter_context
                outk = builder(S, nc, BE, baps, ps, pre=pre)
                fence_extra = []
                if bi == 0:
                    mdk = emit_l0ffn_mbuild(S, nc, BE,
                                            {"RM": ext["RM"], "RUP": ext["RUP"],
                                             "CIDENT": ext["CIDENT"]},
                                            ps, MD, pre="gm")
                    blocks[1][2]["MD_PREBUILT"] = (MD, mdk)
                    fence_extra = mdk
            if outnames is not None:
                S.dram_deps[outnames[0]] = list(outk)
                S.dram_deps[outnames[1]] = list(outk)
            barrier(list(outk) + fence_extra, f"b{bi}")
        S.emit()
    return nc


_FUSED_CACHE = {}


def kernel(L0_attn_x, L0_attn_Wq, L0_attn_Wk, L0_attn_Wv, L0_attn_Wo, L0_attn_bo,
           L0_ffn_rail_maker, L0_ffn_rail_up_proj, L0_ffn_w_cproj,
           L1_attn_W_qkv, L1_attn_b_qkv,
           L1_ffn_kernel, L1_ffn_bias, L1_ffn_a, L1_ffn_b):
    consts = host_constants()
    ncores = 8
    x = np.ascontiguousarray(np.asarray(L0_attn_x, dtype=np.float32))
    if "nc" not in _FUSED_CACHE:
        _FUSED_CACHE["nc"] = build_fused_program()
    nc = _FUSED_CACHE["nc"]
    common = {"WQ": np.asarray(L0_attn_Wq), "WK": np.asarray(L0_attn_Wk),
              "WV": np.asarray(L0_attn_Wv), "WO": np.asarray(L0_attn_Wo),
              "BO": np.asarray(L0_attn_bo), "RM": np.asarray(L0_ffn_rail_maker),
              "RUP": np.asarray(L0_ffn_rail_up_proj),
              "WCP": np.asarray(L0_ffn_w_cproj),
              "WQKV": np.asarray(L1_attn_W_qkv), "BQKV": np.asarray(L1_attn_b_qkv),
              "KER": np.asarray(L1_ffn_kernel), "BIAS": np.asarray(L1_ffn_bias),
              "Aw": np.asarray(L1_ffn_a), "Bw": np.asarray(L1_ffn_b),
              "CONESROW": consts["CONESROW"], "CIDENT": consts["CIDENT"],
              "CROPE_C": consts["CROPE_C"], "CROPE_S": consts["CROPE_S"],
              "CROPE_PERM": consts["CROPE_PERM"]}
    maps = [{"X": x[c], **common} for c in range(ncores)]
    res = run_bass_kernel_spmd(nc, maps, core_ids=list(range(ncores)))
    return np.stack([np.asarray(r["OUT"]) for r in res.results], axis=0)


def kernel_unfused(L0_attn_x, L0_attn_Wq, L0_attn_Wk, L0_attn_Wv, L0_attn_Wo, L0_attn_bo,
           L0_ffn_rail_maker, L0_ffn_rail_up_proj, L0_ffn_w_cproj,
           L1_attn_W_qkv, L1_attn_b_qkv,
           L1_ffn_kernel, L1_ffn_bias, L1_ffn_a, L1_ffn_b):
    consts = host_constants()
    ncores = 8
    x = np.ascontiguousarray(np.asarray(L0_attn_x, dtype=np.float32))
    progs = dict(_get_programs())

    common = {"CONESROW": consts["CONESROW"], "CIDENT": consts["CIDENT"]}
    maps = [{"X": x[c], "WQ": np.asarray(L0_attn_Wq), "WK": np.asarray(L0_attn_Wk),
             "WV": np.asarray(L0_attn_Wv), "WO": np.asarray(L0_attn_Wo),
             "BO": np.asarray(L0_attn_bo), **common} for c in range(ncores)]
    res = run_bass_kernel_spmd(progs["l0attn"], maps, core_ids=list(range(ncores)))
    x1 = [np.asarray(r["X_OUT"]) for r in res.results]
    x1b = [np.asarray(r["XBF_OUT"]) for r in res.results]
    maps = [{"X": x1[c], "XBF": x1b[c], "RM": np.asarray(L0_ffn_rail_maker),
             "RUP": np.asarray(L0_ffn_rail_up_proj),
             "WCP": np.asarray(L0_ffn_w_cproj), **common} for c in range(ncores)]
    res = run_bass_kernel_spmd(progs["l0ffn"], maps, core_ids=list(range(ncores)))
    x2 = [np.asarray(r["X_OUT"]) for r in res.results]
    x2b = [np.asarray(r["XBF_OUT"]) for r in res.results]
    maps = [{"X": x2[c], "XBF": x2b[c], "WQKV": np.asarray(L1_attn_W_qkv),
             "BQKV": np.asarray(L1_attn_b_qkv), "CROPE_C": consts["CROPE_C"],
             "CROPE_S": consts["CROPE_S"], "CROPE_PERM": consts["CROPE_PERM"],
             **common} for c in range(ncores)]
    res = run_bass_kernel_spmd(progs["l1attn"], maps, core_ids=list(range(ncores)))
    x3 = [np.asarray(r["X_OUT"]) for r in res.results]
    x3b = [np.asarray(r["XBF_OUT"]) for r in res.results]
    maps = [{"X": x3[c], "XBF": x3b[c], "KER": np.asarray(L1_ffn_kernel),
             "BIAS": np.asarray(L1_ffn_bias), "Aw": np.asarray(L1_ffn_a),
             "Bw": np.asarray(L1_ffn_b), **common} for c in range(ncores)]
    res = run_bass_kernel_spmd(progs["l1ffn"], maps, core_ids=list(range(ncores)))
    out = np.stack([np.asarray(r["X_OUT"]) for r in res.results], axis=0)
    return out
